# version 04 (final)
# baseline (speedup 1.0000x reference)
import numpy as np
import ml_dtypes
from contextlib import ExitStack

import concourse.bass as bass
import concourse.mybir as mybir
from concourse.bass_utils import run_bass_kernel_spmd

F32 = mybir.dt.float32
BF16 = mybir.dt.bfloat16
I32 = mybir.dt.int32
AF = mybir.ActivationFunctionType
ALU = mybir.AluOpType
AX = mybir.AxisListType
BF = ml_dtypes.bfloat16

B, T, D = 4, 4096, 1024
NLOC = 16
TLOC = NLOC * 128
W_OV = NLOC * 3
WTOT = TLOC + W_OV
L0_H, L0_HD = 16, 64
L0_I, L0_IP = 2764, 2816
MOE_DFF = 2048
L1_HD = 32
L1_FF = 3072
LAMBDA_INIT = float(0.8 - 0.6 * np.exp(-0.3 * 12))
GATE_BIAS = 4.0
NEG = -1e30

ENGINES = ("sync", "tensor", "vector", "scalar", "gpsimd")
SAME_ENGINE_SAFE = {"tensor": True, "sync": False, "vector": False,
                    "scalar": False, "gpsimd": False}


class Prog:
    NSLOT = 8

    def __init__(self, nc):
        self.nc = nc
        self.ops = []
        self.last_write = {}
        self.readers = {}
        self._fence_key = None
        self._fence_n = 0
        self._dma_count = {}
        self._slot_last = {}

    def _names(self, aps):
        out = []
        for a in aps:
            if a is None:
                continue
            out.append(a if isinstance(a, str)
                       else (a.tensor.name if hasattr(a, "tensor") else a.name))
        return out

    def op(self, engine, fn, reads=(), writes=(), inc=1, tag=""):
        reads = self._names(reads)
        writes = self._names(writes)
        if engine in ("vector", "scalar"):
            bank_reads = [b for b in reads if b.startswith("bank")]
            for b in bank_reads:
                reads.remove(b)
                writes.append(b)
        if self._fence_key is not None:
            reads.append(self._fence_key)
        deps = set()
        for b in reads:
            if b in self.last_write:
                deps.add(self.last_write[b])
        for b in writes:
            if b in self.last_write:
                deps.add(self.last_write[b])
            for r in self.readers.get(b, ()):
                deps.add(r)
        i = len(self.ops)
        sem_key = engine
        if inc == 16:
            n = self._dma_count.get(engine, 0)
            self._dma_count[engine] = n + 1
            slot = n % self.NSLOT
            sem_key = f"{engine}_d{slot}"
            prev = self._slot_last.get(sem_key)
            if prev is not None:
                deps.add(prev)
            self._slot_last[sem_key] = i
        for b in reads:
            self.readers.setdefault(b, []).append(i)
        for b in writes:
            self.last_write[b] = i
            self.readers[b] = []
        self.ops.append(dict(engine=engine, fn=fn, deps=deps, inc=inc,
                             need_inc=False, cum=None, tag=tag,
                             sem_key=sem_key))
        return i

    def fence(self):
        lasts = {}
        for i, o in enumerate(self.ops):
            lasts[o["engine"]] = i
        self._fence_n += 1
        key = f"__fence{self._fence_n}__"
        deps = set(lasts.values())
        for e in ENGINES:
            if e not in lasts:
                continue
        def fn(eng):
            return eng.nop()
        i = len(self.ops)
        self.ops.append(dict(engine="gpsimd", fn=fn, deps=deps, inc=1,
                             need_inc=False, cum=None, tag="fence",
                             sem_key="gpsimd"))
        self.last_write[key] = i
        self.readers[key] = []
        self._fence_key = key

    def dma(self, queue, out, in_, tag="dma"):
        def fn(eng, out=out, in_=in_):
            return eng.dma_start(out=out, in_=in_)
        self.op(queue, fn, reads=[in_], writes=[out], inc=16, tag=tag)

    def mm(self, out, lhsT, rhs, start, stop, tile_position=None, tag="mm"):
        def fn(eng, out=out, lhsT=lhsT, rhs=rhs, start=start, stop=stop,
               tp=tile_position):
            return eng.matmul(out, lhsT, rhs, start=start, stop=stop,
                              tile_position=tp)
        self.op("tensor", fn, reads=[lhsT, rhs], writes=[out], tag=tag)

    def transpose(self, out, in_, ident, tag="tr"):
        def fn(eng, out=out, in_=in_, ident=ident):
            return eng.transpose(out, in_, ident)
        self.op("tensor", fn, reads=[in_, ident], writes=[out], tag=tag)

    def act(self, out, in_, func, bias=0.0, scale=1.0, accum_out=None, tag="act"):
        reads = [in_] + [x for x in (bias, scale) if hasattr(x, "tensor")]
        writes = [out] + ([accum_out] if accum_out is not None else [])
        def fn(eng, out=out, in_=in_, func=func, bias=bias, scale=scale,
               accum_out=accum_out):
            return eng.activation(out, in_, func, bias=bias, scale=scale,
                                  accum_out=accum_out)
        self.op("scalar", fn, reads=reads, writes=writes, tag=tag)

    def scopy(self, out, in_, tag="scopy"):
        def fn(eng, out=out, in_=in_):
            return eng.copy(out, in_)
        self.op("scalar", fn, reads=[in_], writes=[out], tag=tag)

    def tt(self, out, a, b, alu, engine="vector", tag="tt"):
        def fn(eng, out=out, a=a, b=b, alu=alu):
            return eng.tensor_tensor(out, a, b, alu)
        self.op(engine, fn, reads=[a, b], writes=[out], tag=tag)

    def ts(self, out, in0, s1, s2=None, op0=ALU.mult, op1=None,
           engine="vector", tag="ts"):
        reads = [in0] + [x for x in (s1, s2) if hasattr(x, "tensor")]
        def fn(eng, out=out, in0=in0, s1=s1, s2=s2, op0=op0, op1=op1):
            if op1 is None:
                return eng.tensor_scalar(out, in0, s1, None, op0)
            return eng.tensor_scalar(out, in0, s1, s2, op0, op1)
        self.op(engine, fn, reads=reads, writes=[out], tag=tag)

    def stt(self, out, in0, scalar, in1, op0, op1, engine="vector", tag="stt"):
        reads = [in0, in1] + ([scalar] if hasattr(scalar, "tensor") else [])
        def fn(eng, out=out, in0=in0, scalar=scalar, in1=in1, op0=op0, op1=op1):
            return eng.scalar_tensor_tensor(out, in0, scalar, in1, op0, op1)
        self.op(engine, fn, reads=reads, writes=[out], tag=tag)

    def copy(self, out, in_, engine="vector", tag="copy"):
        if engine == "scalar":
            def fn(eng, out=out, in_=in_):
                return eng.copy(out, in_)
        else:
            def fn(eng, out=out, in_=in_):
                return eng.tensor_copy(out, in_)
        self.op(engine, fn, reads=[in_], writes=[out], tag=tag)

    def memset(self, ap, val, engine="vector", tag="memset"):
        def fn(eng, ap=ap, val=val):
            return eng.memset(ap, val)
        self.op(engine, fn, reads=[], writes=[ap], tag=tag)

    def reduce(self, out, in_, alu, engine="vector", tag="red"):
        def fn(eng, out=out, in_=in_, alu=alu):
            return eng.tensor_reduce(out, in_, AX.X, alu)
        self.op(engine, fn, reads=[in_], writes=[out], tag=tag)

    def recip(self, out, in_, tag="recip"):
        def fn(eng, out=out, in_=in_):
            return eng.reciprocal(out, in_)
        self.op("vector", fn, reads=[in_], writes=[out], tag=tag)

    def cc(self, kind, alu, groups, ins, outs, tag="cc"):
        def fn(eng, kind=kind, alu=alu, groups=groups, ins=ins, outs=outs):
            return eng.collective_compute(kind, alu, replica_groups=groups,
                                          ins=[a.opt() for a in ins],
                                          outs=[a.opt() for a in outs])
        self.op("gpsimd", fn, reads=list(ins), writes=list(outs), inc=1, tag=tag)

    def assemble(self, block, make_sem):
        ops = self.ops
        for o in ops:
            e = o["engine"]
            for d in o["deps"]:
                p = ops[d]
                if p["engine"] == e and p["sem_key"] == e and SAME_ENGINE_SAFE[e]:
                    continue
                p["need_inc"] = True
        for o in ops:
            if o["inc"] == 16:
                o["need_inc"] = True
        keys = sorted({o["sem_key"] for o in ops} | set(ENGINES))
        sems = {k: make_sem(k) for k in keys}
        cum = {k: 0 for k in keys}
        for o in ops:
            if o["need_inc"]:
                cum[o["sem_key"]] += o["inc"]
                o["cum"] = cum[o["sem_key"]]
        self.final_cum = dict(cum)
        clock = {e: {k: 0 for k in keys} for e in ENGINES}
        snapshots = [None] * len(ops)
        running = {k: 0 for k in keys}
        for i, o in enumerate(ops):
            e = o["engine"]
            waits = {}
            for d in o["deps"]:
                p = ops[d]
                k = p["sem_key"]
                if p["engine"] == e and k == e and SAME_ENGINE_SAFE[e]:
                    continue
                v = p["cum"]
                if v is None or clock[e][k] >= v:
                    continue
                waits[k] = max(waits.get(k, 0), v)
            o["waits"] = sorted(waits.items())
            for k, v in o["waits"]:
                if clock[e][k] < v:
                    clock[e][k] = v
                snap = None
                for d in o["deps"]:
                    p = ops[d]
                    if p["sem_key"] == k and p["cum"] == v:
                        snap = snapshots[d]
                if snap:
                    for g in keys:
                        if clock[e][g] < snap[g]:
                            clock[e][g] = snap[g]
            if o["need_inc"]:
                running[o["sem_key"]] = o["cum"]
            snap = dict(clock[e])
            for k in keys:
                if running[k] > snap[k] and (k == e or k.startswith(e + "_d")):
                    pass
            snap[o["sem_key"]] = max(snap[o["sem_key"]],
                                     o["cum"] if o["cum"] is not None
                                     else running[o["sem_key"]])
            snapshots[i] = snap

        per_engine = {e: [] for e in ENGINES}
        for o in ops:
            per_engine[o["engine"]].append(o)

        def emitter_for(name):
            def emitter(eng):
                for o in per_engine[name]:
                    for k, v in o["waits"]:
                        eng.wait_ge(sems[k], v)
                    inst = o["fn"](eng)
                    if o["need_inc"]:
                        inst.then_inc(sems[o["sem_key"]], o["inc"])
            return emitter

        block.sync(emitter_for("sync"))
        block.tensor(emitter_for("tensor"))
        block.vector(emitter_for("vector"))
        block.scalar(emitter_for("scalar"))

        def gps(eng):
            emitter_for("gpsimd")(eng)
            for k in keys:
                if k != "gpsimd" and self.final_cum[k] > 0:
                    eng.wait_ge(sems[k], self.final_cum[k])
        block.gpsimd(gps)


class Ctx(dict):
    def ap(self, name):
        return self[name].ap()


def make_nc():
    return bass.Bass("TRN2", target_bir_lowering=False)


def build_launch(emit_fn, core_inputs, n_cores=8, io_specs=None):
    raise NotImplementedError



def tok_index(p):
    idx = np.empty(TLOC, dtype=np.int64)
    for j in range(NLOC):
        g = 2 * j + p
        idx[j * 128:(j + 1) * 128] = np.arange(g * 128, g * 128 + 128)
    return idx


def shard_x(x, core_id):
    b, p = divmod(core_id, 2)
    xb = x[b]
    xt = np.ascontiguousarray(xb[tok_index(p)].T.astype(np.float32))
    ov = np.zeros((W_OV, D), np.float32)
    for j in range(NLOC):
        g = 2 * j + p
        s0 = g * 128 - 3
        for k in range(3):
            if s0 + k >= 0:
                ov[j * 3 + k] = xb[s0 + k]
    ovt = np.ascontiguousarray(ov.T.astype(np.float32))
    return xt, ovt


def host_consts(core_id):
    b, p = divmod(core_id, 2)
    c = {}
    c["ident_bf"] = np.eye(128, dtype=BF)
    c["ident_f32"] = np.eye(128, dtype=np.float32)
    c["ones_col"] = np.ones((128, 1), dtype=BF)
    c["ones_row"] = np.ones((1, 128), dtype=BF)
    t = np.arange(128)[:, None]
    s = np.arange(128)[None, :]
    c["maskneg_sl_ts"] = -(s < t).astype(np.float32)
    c["mask_li_st"] = (t <= s).astype(np.float32)
    c["cum_li"] = (t <= s).astype(np.float32)
    c["maskneg_sl_ts"] = c["maskneg_sl_ts"].astype(BF)
    c["mask_li_st"] = c["mask_li_st"].astype(BF)
    c["cum_li"] = c["cum_li"].astype(BF)
    c["triu01_kq"] = (t <= s).astype(BF)
    pos = tok_index(p).astype(np.float32)
    inv_freq = 1.0 / (10000.0 ** (np.arange(0, L1_HD, 2, dtype=np.float32) / L1_HD))
    fr = pos[:, None] * inv_freq[None, :]
    cosv, sinv = np.cos(fr), np.sin(fr)
    C = np.empty((128, TLOC), np.float32)
    S = np.empty((128, TLOC), np.float32)
    for a in range(4):
        for j in range(32):
            C[32 * a + j] = cosv[:, j % 16]
            S[32 * a + j] = sinv[:, j % 16] if j < 16 else -sinv[:, j % 16]
    sc = 1.0 / np.sqrt(np.float32(L1_HD))
    c["ropeCq"], c["ropeSq"] = C * sc, S * sc
    c["ropeCk"], c["ropeSk"] = C, S
    perm = np.zeros((128, 128), dtype=BF)
    for a in range(4):
        for j in range(16):
            perm[32 * a + j, 32 * a + j + 16] = 1
            perm[32 * a + j + 16, 32 * a + j] = 1
    c["rope_perm"] = perm
    for fc in range(8):
        pat = np.zeros((128, 16), dtype=BF)
        for pp in range(128):
            pat[pp, (fc * 128 + pp) // L0_HD] = 1
        c[f"pat16_{fc}"] = pat
        c[f"bc16_{fc}"] = np.ascontiguousarray(pat.T)
        pat32 = np.zeros((128, 32), dtype=BF)
        for pp in range(128):
            pat32[pp, (fc * 128 + pp) // L1_HD] = 1
        c[f"pat32_{fc}"] = pat32
    return c



def alloc_banks(P, ctx):
    for i in range(8):
        tname = f"bank{i}"
        ctx[tname] = ctx["stack"].enter_context(
            ctx["nc"].psum_tensor(tname, [128, 512], F32))


def sb(ctx, name, shape, dtype):
    ctx[name] = ctx["stack"].enter_context(
        ctx["nc"].sbuf_tensor(name, shape, dtype))
    return ctx[name]



def sb(ctx, name, shape, dtype):
    ctx[name] = ctx["stack"].enter_context(
        ctx["nc"].sbuf_tensor(name, shape, dtype))
    return ctx[name]


def alloc_banks(ctx):
    for i in range(8):
        ctx[f"bank{i}"] = ctx["stack"].enter_context(
            ctx["nc"].psum_tensor(f"bank{i}", [128, 512], F32))


CONST_SPECS = {
    "ident_bf": ([128, 128], BF16), "ident_f32": ([128, 128], F32),
    "ones_col": ([128, 1], BF16), "ones_row": ([1, 128], BF16),
    "maskneg_sl_ts": ([128, 128], BF16), "mask_li_st": ([128, 128], BF16),
    "cum_li": ([128, 128], BF16), "triu01_kq": ([128, 128], BF16),
    "ropeCq": ([128, TLOC], F32), "ropeSq": ([128, TLOC], F32),
    "ropeCk": ([128, TLOC], F32), "ropeSk": ([128, TLOC], F32),
    "rope_perm": ([128, 128], BF16),
}
for _fc in range(8):
    CONST_SPECS[f"pat16_{_fc}"] = ([128, 16], BF16)
    CONST_SPECS[f"bc16_{_fc}"] = ([16, 128], BF16)
    CONST_SPECS[f"pat32_{_fc}"] = ([128, 32], BF16)


def declare_consts(P, ctx, names):
    for n in names:
        shape, dt = CONST_SPECS[n]
        d = ctx["nc"].dram_tensor("c_" + n, shape, dt, kind="ExternalInput")
        ctx["c_" + n] = d
        sb(ctx, n, shape, dt)
        P.dma("sync", ctx[n].ap(), d.ap())


def const_col(P, ctx, value):
    key = ("cpool", float(value))
    if "cpool_vals" not in ctx:
        ctx["cpool_vals"] = {}
        sb(ctx, "cpool", [128, 16], F32)
    vals = ctx["cpool_vals"]
    if key not in vals:
        i = len(vals)
        vals[key] = i
        P.memset(ctx["cpool"].ap()[:, i:i + 1], float(value), engine="vector")
    return ctx["cpool"].ap()[:, vals[key]:vals[key] + 1]


def sb_tmp(ctx, name, shape, dtype):
    ctx[name] = ctx["tmpstack"].enter_context(
        ctx["nc"].sbuf_tensor(name, shape, dtype, side="right"))
    return ctx[name]


def _ones_f(P, ctx):
    if "ones_colf" not in ctx:
        sb(ctx, "ones_colf", [128, 1], F32)
        P.memset(ctx["ones_colf"].ap(), 1.0, engine="vector")
        sb(ctx, "xsqf", [128, 512], F32)
    return ctx["ones_colf"].ap()


def emit_pre(P, ctx):
    R = [ctx[f"R{fc}"] for fc in range(8)]
    Rov = ctx["Rov"]
    sb_tmp(ctx, "xbf", [128, 512], BF16)
    sb_tmp(ctx, "xsq", [128, 512], BF16)
    sb_tmp(ctx, "tmpf", [128, 512], F32)
    for nm in ("r_a", "r_b", "r_v", "r_s", "r_n"):
        sb_tmp(ctx, nm, [1, 512], F32)
    sb_tmp(ctx, "r_sbf", [1, 512], BF16)
    sb_tmp(ctx, "r_nbf", [1, 512], BF16)
    for fc in range(8):
        sb_tmp(ctx, f"xn{fc}", [128, TLOC], BF16)
    sb_tmp(ctx, "xnov", [128, 8 * W_OV], BF16)
    sb_tmp(ctx, "xnpad", [128, 131], BF16)
    sb_tmp(ctx, "convt", [128, 128], F32)
    sb(ctx, "convw2", [128, 32], F32)
    for fc in range(8):
        sb(ctx, f"xc{fc}", [128, TLOC], BF16)

    b_sum, b_sq, b_s, b_m = (ctx["bank0"].ap(), ctx["bank1"].ap(),
                             ctx["bank2"].ap(), ctx["bank3"].ap())
    ones_col = ctx["ones_col"].ap()
    ones_row = ctx["ones_row"].ap()

    for fc in range(8):
        P.ts(ctx["convw2"].ap()[:, 4 * fc:4 * fc + 4],
             ctx["convw"].ap()[:, 4 * fc:4 * fc + 4],
             ctx["n1w_col"].ap()[:, fc:fc + 1], op0=ALU.mult, engine="gpsimd")

    onesf = _ones_f(P, ctx)
    chunks = [(cc * 512, 512, False) for cc in range(4)] + [(0, W_OV, True)]
    for c0, w, is_ov in chunks:
        for fc in range(8):
            src = (Rov.ap()[:, fc * W_OV:(fc + 1) * W_OV] if is_ov
                   else R[fc].ap()[:, c0:c0 + w])
            P.act(ctx["xsqf"].ap()[:, 0:w], src, AF.Square)
            P.mm(b_sum[0:1, 0:w], onesf, src,
                 start=(fc == 0), stop=(fc == 7))
            P.mm(b_sq[0:1, 0:w], onesf, ctx["xsqf"].ap()[:, 0:w],
                 start=(fc == 0), stop=(fc == 7))
        r_a, r_b, r_v, r_s, r_n = (ctx[nm].ap()[0:1, 0:w] for nm in
                                   ("r_a", "r_b", "r_v", "r_s", "r_n"))
        P.ts(r_a, b_sum[0:1, 0:w], 1.0 / D, op0=ALU.mult)
        P.tt(r_b, r_a, r_a, ALU.mult)
        P.ts(r_s, b_sq[0:1, 0:w], 1.0 / D, op0=ALU.mult)
        P.tt(r_v, r_s, r_b, ALU.subtract)
        P.ts(r_s, r_v, 1e-5, op0=ALU.add)
        P.recip(r_n, r_s)
        P.tt(r_s, r_v, r_n, ALU.mult)
        P.ts(r_s, r_s, 1e-6, op0=ALU.add)
        P.recip(r_b, r_s)
        P.tt(r_s, r_n, r_b, ALU.mult)
        P.act(r_s, r_s, AF.Sqrt)
        P.tt(r_n, r_a, r_s, ALU.mult)
        P.ts(r_n, r_n, -1.0, op0=ALU.mult)
        P.copy(ctx["r_sbf"].ap()[0:1, 0:w], r_s, engine="vector")
        P.copy(ctx["r_nbf"].ap()[0:1, 0:w], r_n, engine="vector")
        P.mm(b_s[:, 0:w], ones_row, ctx["r_sbf"].ap()[0:1, 0:w],
             start=True, stop=True)
        P.mm(b_m[:, 0:w], ones_row, ctx["r_nbf"].ap()[0:1, 0:w],
             start=True, stop=True)
        for fc in range(8):
            src = (Rov.ap()[:, fc * W_OV:(fc + 1) * W_OV] if is_ov
                   else R[fc].ap()[:, c0:c0 + w])
            dst = (ctx["xnov"].ap()[:, fc * W_OV:(fc + 1) * W_OV] if is_ov
                   else ctx[f"xn{fc}"].ap()[:, c0:c0 + w])
            P.tt(ctx["tmpf"].ap()[:, 0:w], src, b_s[:, 0:w], ALU.mult)
            P.tt(dst, ctx["tmpf"].ap()[:, 0:w], b_m[:, 0:w], ALU.add)

    cw = ctx["convw2"].ap()
    for fc in range(8):
        for j in range(NLOC):
            P.copy(ctx["xnpad"].ap()[:, 0:3],
                   ctx["xnov"].ap()[:, fc * W_OV + 3 * j:fc * W_OV + 3 * j + 3],
                   engine="gpsimd")
            P.copy(ctx["xnpad"].ap()[:, 3:131],
                   ctx[f"xn{fc}"].ap()[:, j * 128:(j + 1) * 128],
                   engine="gpsimd")
            P.ts(ctx["convt"].ap(), ctx["xnpad"].ap()[:, 0:128],
                 cw[:, 4 * fc:4 * fc + 1], op0=ALU.mult, engine="vector")
            for tap in (1, 2, 3):
                P.stt(ctx["convt"].ap(), ctx["xnpad"].ap()[:, tap:tap + 128],
                      cw[:, 4 * fc + tap:4 * fc + tap + 1], ctx["convt"].ap(),
                      ALU.mult, ALU.add, engine="vector")
            P.act(ctx[f"xc{fc}"].ap()[:, j * 128:(j + 1) * 128],
                  ctx["convt"].ap(), AF.Silu,
                  bias=ctx["convb"].ap()[:, fc:fc + 1])


def load_weight_bf16(P, ctx, dram, dst_name, kcs, cols,
                     conv_engines=("vector", "scalar", "gpsimd")):
    if "_wld0" not in ctx:
        sb(ctx, "_wld0", [128, 1024], F32)
        sb(ctx, "_wld1", [128, 1024], F32)
    dst = ctx[dst_name]
    i = 0
    for kc in range(kcs):
        ncols = 0
        while ncols < cols:
            wv = min(1024, cols - ncols)
            buf = ctx[f"_wld{i % 2}"]
            P.dma("sync", buf.ap()[:, 0:wv],
                  dram.ap()[kc * 128:(kc + 1) * 128, ncols:ncols + wv])
            eng = conv_engines[i % len(conv_engines)]
            if eng == "scalar":
                P.scopy(dst.ap()[:, kc * cols + ncols:kc * cols + ncols + wv],
                        buf.ap()[:, 0:wv])
            else:
                P.copy(dst.ap()[:, kc * cols + ncols:kc * cols + ncols + wv],
                       buf.ap()[:, 0:wv], engine=eng)
            ncols += wv
            i += 1


def emit_l0proj(P, ctx, q_out, k_out, v_out):
    xc = [ctx[f"xc{fc}"] for fc in range(8)]
    sb(ctx, "wqb", [128, 8 * 1024], BF16)
    sb(ctx, "wkb", [128, 8 * 1024], BF16)
    sb(ctx, "wvb", [128, 8 * 1024], BF16)
    sb(ctx, "wbab", [128, 8 * 48], BF16)
    load_weight_bf16(P, ctx, ctx["in_wq"], "wqb", 8, 1024)
    load_weight_bf16(P, ctx, ctx["in_wk"], "wkb", 8, 1024)
    load_weight_bf16(P, ctx, ctx["in_wv"], "wvb", 8, 1024)
    load_weight_bf16(P, ctx, ctx["in_wba"], "wbab", 8, 48)
    sb(ctx, "praw", [128, 8 * 512], BF16)
    sb(ctx, "beta16", [16, TLOC], F32)
    sb(ctx, "lna16", [16, TLOC], F32)
    sb(ctx, "sqs", [128, 512], BF16)
    sb(ctx, "r16", [16, 512], F32)
    sb(ctx, "r16bf", [16, 512], BF16)

    for cc in range(4):
        bk = ctx[f"bank{cc % 2}"].ap()
        for kc in range(8):
            P.mm(bk[0:48, 0:512], ctx["wbab"].ap()[:, kc * 48:(kc + 1) * 48],
                 xc[kc].ap()[:, cc * 512:(cc + 1) * 512],
                 start=(kc == 0), stop=(kc == 7))
        P.act(ctx["beta16"].ap()[:, cc * 512:(cc + 1) * 512], bk[0:16, 0:512],
              AF.Sigmoid)
        P.act(ctx["lna16"].ap()[:, cc * 512:(cc + 1) * 512], bk[32:48, 0:512],
              AF.Sigmoid, bias=const_col(P, ctx, GATE_BIAS)[0:16, :])
    for cc in range(4):
        P.act(ctx["lna16"].ap()[:, cc * 512:(cc + 1) * 512],
              ctx["lna16"].ap()[:, cc * 512:(cc + 1) * 512], AF.Ln)

    for wname, outd in (("wqb", q_out), ("wkb", k_out)):
        w = ctx[wname].ap()
        for cc in range(4):
            bkS = ctx["bank4"].ap()
            for fc in range(8):
                bk = ctx[f"bank{fc % 2}"].ap()
                for kc in range(8):
                    P.mm(bk[:, 0:512],
                         w[:, kc * 1024 + fc * 128:kc * 1024 + (fc + 1) * 128],
                         xc[kc].ap()[:, cc * 512:(cc + 1) * 512],
                         start=(kc == 0), stop=(kc == 7))
                pr = ctx["praw"].ap()[:, fc * 512:(fc + 1) * 512]
                P.copy(pr, bk[:, 0:512], engine="vector")
                P.tt(ctx["sqs"].ap(), pr, pr, ALU.mult, engine="gpsimd")
                P.mm(bkS[0:16, 0:512], ctx[f"pat16_{fc}"].ap(),
                     ctx["sqs"].ap(), start=(fc == 0), stop=(fc == 7))
            P.ts(ctx["r16"].ap(), bkS[0:16, 0:512], 1e-6, op0=ALU.add)
            P.recip(ctx["r16"].ap(), ctx["r16"].ap())
            P.act(ctx["r16"].ap(), ctx["r16"].ap(), AF.Sqrt)
            P.copy(ctx["r16bf"].ap(), ctx["r16"].ap(), engine="vector")
            for fc in range(8):
                bkB = ctx[f"bank{5 + fc % 2}"].ap()
                pr = ctx["praw"].ap()[:, fc * 512:(fc + 1) * 512]
                P.mm(bkB[:, 0:512], ctx[f"bc16_{fc}"].ap(), ctx["r16bf"].ap(),
                     start=True, stop=True)
                P.tt(pr, pr, bkB[:, 0:512], ALU.mult)
                P.dma("gpsimd", outd.ap()[fc * 128:(fc + 1) * 128,
                                          cc * 512:(cc + 1) * 512], pr)

    for tc in range(NLOC):
        for vc in range(2):
            bk = ctx[f"bank{vc}"].ap()
            for kc in range(8):
                P.mm(bk[:, 0:512],
                     xc[kc].ap()[:, tc * 128:(tc + 1) * 128],
                     ctx["wvb"].ap()[:, kc * 1024 + vc * 512:kc * 1024 + (vc + 1) * 512],
                     start=(kc == 0), stop=(kc == 7))
            pr = ctx["praw"].ap()[:, 0:512] if vc == 0 else ctx["praw"].ap()[:, 512:1024]
            P.copy(pr, bk[:, 0:512], engine="vector")
            P.dma("gpsimd", v_out.ap()[tc * 128:(tc + 1) * 128,
                                       vc * 512:(vc + 1) * 512], pr)



def open_launch():
    nc = make_nc()
    ctx = Ctx(nc=nc)
    P = Prog(nc)
    return nc, ctx, P


def finish_launch(nc, ctx, P):
    with ExitStack() as semstack:
        def make_sem(name):
            return semstack.enter_context(nc.semaphore("sem_" + name))
        with nc.Block() as block:
            P.assemble(block, make_sem)
    return nc


def build_launch1():
    nc, ctx, P = open_launch()
    stack = ExitStack()
    ctx["stack"] = stack
    with stack:
        ins = {}
        for nm, shape, dt in [("xT", [D, TLOC], F32), ("xov", [D, W_OV], F32),
                              ("convw", [128, 32], F32), ("convb", [128, 8], F32),
                              ("n1w_col", [128, 8], F32),
                              ("wq", [D, D], F32), ("wk", [D, D], F32),
                              ("wv", [D, D], F32), ("wba", [D, 48], F32)]:
            ins[nm] = nc.dram_tensor("i_" + nm, shape, dt, kind="ExternalInput")
            ctx["in_" + nm] = ins[nm]
        outs = {}
        for nm, shape, dt in [("q_dram", [D, TLOC], BF16),
                              ("k_dram", [D, TLOC], BF16),
                              ("v_dram", [TLOC, D], BF16),
                              ("beta_dram", [16, TLOC], F32),
                              ("lna_dram", [16, TLOC], F32),
                              ("xc_dram", [D, TLOC], BF16)]:
            outs[nm] = nc.dram_tensor("o_" + nm, shape, dt, kind="ExternalOutput")
        alloc_banks(ctx)
        declare_consts(P, ctx, ["ones_col", "ones_row"] +
                       [f"pat16_{fc}" for fc in range(8)] +
                       [f"bc16_{fc}" for fc in range(8)])
        for nm, shape in [("convw", [128, 32]), ("convb", [128, 8]),
                          ("n1w_col", [128, 8])]:
            sb(ctx, nm, shape, F32)
            P.dma("sync", ctx[nm].ap(), ins[nm].ap())
        with ExitStack() as tmpstack:
            ctx["tmpstack"] = tmpstack
            for fc in range(8):
                sb_tmp(ctx, f"R{fc}", [128, TLOC], F32)
                P.dma("sync", ctx[f"R{fc}"].ap(),
                      ins["xT"].ap()[fc * 128:(fc + 1) * 128, :])
            sb_tmp(ctx, "Rov", [128, 8 * W_OV], F32)
            for fc in range(8):
                P.dma("sync", ctx["Rov"].ap()[:, fc * W_OV:(fc + 1) * W_OV],
                      ins["xov"].ap()[fc * 128:(fc + 1) * 128, :])
            emit_pre(P, ctx)
        P.fence()
        emit_l0proj(P, ctx, outs["q_dram"], outs["k_dram"], outs["v_dram"])
        P.dma("gpsimd", outs["beta_dram"].ap(), ctx["beta16"].ap())
        P.dma("gpsimd", outs["lna_dram"].ap(), ctx["lna16"].ap())
        for fc in range(8):
            P.dma("gpsimd", outs["xc_dram"].ap()[fc * 128:(fc + 1) * 128, :],
                  ctx[f"xc{fc}"].ap())
        finish_launch(nc, ctx, P)
    return nc


def launch1_inputs(inputs, core_id):
    xt, ovt = shard_x(np.asarray(inputs["L0_attn_x"]), core_id)
    n1w = np.asarray(inputs["L0_attn_norm1_w"], np.float32)
    cw = np.asarray(inputs["L0_attn_conv_w"], np.float32)
    cb = np.asarray(inputs["L0_attn_conv_b"], np.float32)
    convw = np.zeros((128, 32), np.float32)
    convb = np.zeros((128, 8), np.float32)
    n1c = np.zeros((128, 8), np.float32)
    for fc in range(8):
        convw[:, 4 * fc:4 * fc + 4] = cw[fc * 128:(fc + 1) * 128]
        convb[:, fc] = cb[fc * 128:(fc + 1) * 128]
        n1c[:, fc] = n1w[fc * 128:(fc + 1) * 128]
    wba = np.zeros((D, 48), np.float32)
    wba[:, 0:16] = np.asarray(inputs["L0_attn_wb"], np.float32)
    wba[:, 32:48] = np.asarray(inputs["L0_attn_wa"], np.float32)
    c = host_consts(core_id)
    m = {"i_xT": xt, "i_xov": ovt, "i_convw": convw, "i_convb": convb,
         "i_n1w_col": n1c,
         "i_wq": np.ascontiguousarray(np.asarray(inputs["L0_attn_wq"], np.float32)),
         "i_wk": np.ascontiguousarray(np.asarray(inputs["L0_attn_wk"], np.float32)),
         "i_wv": np.ascontiguousarray(np.asarray(inputs["L0_attn_wv"], np.float32)),
         "i_wba": wba,
         "c_ones_col": c["ones_col"], "c_ones_row": c["ones_row"]}
    for fc in range(8):
        m[f"c_pat16_{fc}"] = c[f"pat16_{fc}"]
        m[f"c_bc16_{fc}"] = c[f"bc16_{fc}"]
    return m



def emit_scan(P, ctx):
    nc = ctx["nc"]
    sb(ctx, "KI", [128, 128], BF16)
    sb(ctx, "beta_ch", [1, 1024], F32)
    sb(ctx, "lna_ch", [1, 1024], F32)
    P.memset(ctx["KI"].ap(), 0.0, engine="vector")
    for nm in ("r_L", "r_e", "r_nbgl", "r_tmp"):
        sb(ctx, nm, [1, 128], F32)
    sb(ctx, "r_e_bf", [1, 128], BF16)
    sb(ctx, "r_en_bf", [1, 128], BF16)
    sb(ctx, "ln_col", [128, 1], F32)
    sb(ctx, "c_b", [128, 8], F32)
    sb(ctx, "c_neLp", [128, 8], F32)
    sb(ctx, "c_bgl", [128, 8], F32)
    sb(ctx, "en8", [128, 1024], BF16)
    sb(ctx, "ebc8", [128, 1024], F32)
    sb(ctx, "Xa", [128, 1024], F32)
    sb(ctx, "XaT", [128, 1024], F32)
    sb(ctx, "Xb", [128, 1024], F32)
    sb(ctx, "XbT", [128, 1024], F32)
    sb(ctx, "PT", [128, 1024], F32)
    sb(ctx, "PTbf", [128, 1024], BF16)
    sb(ctx, "Bt", [128, 512], BF16)
    sb(ctx, "Ubf", [128, 512], BF16)
    sb(ctx, "WTm", [128, 1024], BF16)
    sb(ctx, "KIT", [128, 512], BF16)
    sb(ctx, "Sst", [64, 512], F32)
    sb(ctx, "Sbf", [64, 512], BF16)
    sb(ctx, "tmA", [128, 512], F32)
    sb(ctx, "tmB", [128, 512], F32)
    P.memset(ctx["Sst"].ap(), 0.0, engine="vector")
    P.memset(ctx["Sbf"].ap(), 0.0, engine="vector")

    ones11 = ctx["ones11_f32"].ap()
    cummask = ctx["cummask_f32"].ap()
    ones_row = ctx["ones_row"].ap()
    ident = ctx["ident_bf"].ap()
    masknegr_f = ctx["maskneg_rep_f"].ap()
    maskli_r = ctx["maskli_rep"].ap()
    ident_rf = ctx["ident_rep_f"].ap()
    identf = ctx["ident_f32"].ap()

    bk = [ctx[f"bank{i}"].ap() for i in range(8)]
    bkbf = [ctx[f"bank{i}"].ap().bitcast(BF16) for i in range(8)]

    for c in range(32):
        par = c % 2
        def qk(which, hh, par=par):
            return ctx[f"{which}ch{par}"].ap()[:, 128 * hh:128 * (hh + 1)]
        P.dma("sync", ctx[f"qch{par}"].ap(),
              ctx["in_q_sc"].ap()[64 * c:64 * (c + 1), :])
        P.dma("sync", ctx[f"kch{par}"].ap(),
              ctx["in_k_sc"].ap()[64 * c:64 * (c + 1), :])
        P.dma("sync", ctx["beta_ch"].ap(),
              ctx["in_beta_fl"].ap()[:, c * 128:(c + 1) * 128])
        P.dma("sync", ctx["lna_ch"].ap(),
              ctx["in_lna_fl"].ap()[:, c * 128:(c + 1) * 128])
        for hh in range(8):
            lnr = ctx["lna_ch"].ap()[0:1, 128 * hh:128 * hh + 128]
            btr = ctx["beta_ch"].ap()[0:1, 128 * hh:128 * hh + 128]
            P.mm(bk[7][:, 0:1], lnr, ones11, start=True, stop=True)
            P.copy(ctx["ln_col"].ap(), bk[7][:, 0:1], engine="vector")
            P.mm(bk[7][0:1, 0:128], ctx["ln_col"].ap(), cummask,
                 start=True, stop=True)
            P.copy(ctx["r_L"].ap(), bk[7][0:1, 0:128], engine="vector")
            P.act(ctx["r_e"].ap(), ctx["r_L"].ap(), AF.Exp)
            P.copy(ctx["r_e_bf"].ap(), ctx["r_e"].ap(), engine="vector")
            P.act(ctx["r_en_bf"].ap(), ctx["r_L"].ap(), AF.Exp, scale=-1.0)
            P.tt(ctx["r_nbgl"].ap(), ctx["r_e"].ap(), btr, ALU.mult)
            P.mm(bk[7][:, 4:5], ctx["r_nbgl"].ap(), ones11, start=True, stop=True)
            P.copy(ctx["c_bgl"].ap()[:, hh:hh + 1], bk[7][:, 4:5], engine="vector")
            P.mm(bk[7][:, 8:9], btr, ones11, start=True, stop=True)
            P.copy(ctx["c_b"].ap()[:, hh:hh + 1], bk[7][:, 8:9], engine="vector")
            P.ts(ctx["r_tmp"].ap(), ctx["r_e"].ap(), -1.0, op0=ALU.mult)
            P.mm(bk[7][:, 12:13], ctx["r_tmp"].ap(), ones11, start=True, stop=True)
            P.copy(ctx["c_neLp"].ap()[:, hh:hh + 1], bk[7][:, 12:13],
                   engine="vector")
            P.mm(bk[6][:, 0:128], ones_row, ctx["r_en_bf"].ap(),
                 start=True, stop=True)
            P.copy(ctx["en8"].ap()[:, 128 * hh:128 * (hh + 1)], bk[6][:, 0:128],
                   engine="scalar")
            P.mm(bk[6][:, 128:256], ones_row, ctx["r_e_bf"].ap(),
                 start=True, stop=True)
            P.copy(ctx["ebc8"].ap()[:, 128 * hh:128 * (hh + 1)], bk[6][:, 128:256],
                   engine="vector")
        for hh in range(8):
            P.tt(ctx["KI"].ap()[0:64, :], qk("k", hh),
                 ctx["en8"].ap()[0:64, 128 * hh:128 * (hh + 1)], ALU.mult,
                 engine="gpsimd")
            b_a = bk[hh % 2]
            qcol = 128 * (hh // 2)
            P.mm(b_a[:, qcol:qcol + 128], qk("k", hh),
                 ctx["KI"].ap()[0:64, :], start=True, stop=True)
            P.ts(ctx["Xa"].ap()[:, 128 * hh:128 * (hh + 1)],
                 b_a[:, qcol:qcol + 128],
                 ctx["c_bgl"].ap()[:, hh:hh + 1], op0=ALU.mult)
            P.transpose(bkbf[2][:, 128 * (hh % 4):128 * (hh % 4) + 128],
                        ctx["KI"].ap(), ident)
            P.copy(ctx["KIT"].ap()[:, 64 * hh:64 * (hh + 1)],
                   bkbf[2][:, 128 * (hh % 4):128 * (hh % 4) + 64],
                   engine="scalar")
            b_w = bk[3 + hh % 2]
            P.mm(b_w[:, qcol:qcol + 128], ctx["KI"].ap()[0:64, :],
                 qk("q", hh), start=True, stop=True)
            P.copy(ctx["WTm"].ap()[:, 128 * hh:128 * (hh + 1)],
                   b_w[:, qcol:qcol + 128], engine="vector")
        for half in range(2):
            hs = slice(512 * half, 512 * (half + 1))
            P.tt(ctx["Xa"].ap()[:, hs], ctx["Xa"].ap()[:, hs],
                 masknegr_f[:, 0:512], ALU.mult, engine="gpsimd")
            P.tt(ctx["WTm"].ap()[:, hs], ctx["WTm"].ap()[:, hs],
                 maskli_r[:, 0:512], ALU.mult, engine="gpsimd")
        for hh in range(8):
            P.transpose(bk[5 + hh // 4][:, 128 * (hh % 4):128 * (hh % 4) + 128],
                        ctx["Xa"].ap()[:, 128 * hh:128 * (hh + 1)], identf)
        for i in range(2):
            P.copy(ctx["XaT"].ap()[:, 512 * i:512 * (i + 1)],
                   bk[5 + i][:, 0:512], engine="vector")
        P.tt(ctx["PT"].ap(), ctx["XaT"].ap(), ident_rf, ALU.add, engine="gpsimd")
        if ctx.get("dbg_chunk") == c and ctx.get("dbg_early"):
            for nm in ("Xa", "XaT", "PT"):
                P.dma("gpsimd", ctx["dbg_" + nm].ap(), ctx[nm].ap())
        X, XT, X2, X2T = "Xa", "XaT", "Xb", "XbT"
        for lev in range(6):
            for hh in range(8):
                b_x = bk[hh // 4]
                qcol = 128 * (hh % 4)
                P.mm(b_x[:, qcol:qcol + 128],
                     ctx[XT].ap()[:, 128 * hh:128 * (hh + 1)],
                     ctx[X].ap()[:, 128 * hh:128 * (hh + 1)],
                     start=True, stop=True)
            for i in range(2):
                P.copy(ctx[X2].ap()[:, 512 * i:512 * (i + 1)], bk[i][:, 0:512],
                       engine="vector")
            for hh in range(8):
                b_p = bk[3 + hh // 4]
                qcol = 128 * (hh % 4)
                P.mm(b_p[:, qcol:qcol + 128],
                     ctx[X2].ap()[:, 128 * hh:128 * (hh + 1)],
                     ctx["PT"].ap()[:, 128 * hh:128 * (hh + 1)],
                     start=True, stop=True)
            for i in range(2):
                P.copy(ctx["tmA"].ap(), bk[3 + i][:, 0:512], engine="scalar")
                P.tt(ctx["PT"].ap()[:, 512 * i:512 * (i + 1)],
                     ctx["PT"].ap()[:, 512 * i:512 * (i + 1)],
                     ctx["tmA"].ap(), ALU.add, engine="vector")
            if lev < 5:
                for hh in range(8):
                    P.transpose(bk[5 + hh // 4][:, 128 * (hh % 4):128 * (hh % 4) + 128],
                                ctx[X2].ap()[:, 128 * hh:128 * (hh + 1)], identf)
                    if hh % 4 == 3:
                        P.copy(ctx[X2T].ap()[:, 512 * (hh // 4):512 * (hh // 4 + 1)],
                               bk[5 + hh // 4][:, 0:512], engine="vector")
            if ctx.get("dbg_chunk") == c and ctx.get("dbg_early") == f"lev{lev}":
                for nm in ("PT", X2, X2T):
                    P.dma("gpsimd", ctx["dbg_" + ("Xa" if nm == X2 else ("XaT" if nm == X2T else nm))].ap(), ctx[nm].ap())
            if lev < 5:
                X, XT, X2, X2T = X2, X2T, X, XT
        if ctx.get("dbg_chunk") == c and not ctx.get("dbg_early"):
            for nm in ("Xa", "XaT", "PT", "WTm", "en8", "ebc8"):
                P.dma("gpsimd", ctx["dbg_" + nm].ap(), ctx[nm].ap())
        P.copy(ctx["PTbf"].ap(), ctx["PT"].ap(), engine="vector")
        vt = ctx["v_sb"].ap()[:, 512 * c:512 * (c + 1)]
        for hh in range(8):
            P.mm(bk[5][:, 64 * hh:64 * (hh + 1)], qk("k", hh),
                 ctx["Sbf"].ap()[:, 64 * hh:64 * (hh + 1)],
                 start=True, stop=True)
            P.stt(ctx["tmA"].ap()[:, 64 * hh:64 * (hh + 1)],
                  bk[5][:, 64 * hh:64 * (hh + 1)],
                  ctx["c_neLp"].ap()[:, hh:hh + 1],
                  vt[:, 64 * hh:64 * (hh + 1)], ALU.mult, ALU.add)
            P.ts(ctx["Bt"].ap()[:, 64 * hh:64 * (hh + 1)],
                 ctx["tmA"].ap()[:, 64 * hh:64 * (hh + 1)],
                 ctx["c_b"].ap()[:, hh:hh + 1], op0=ALU.mult)
        for hh in range(8):
            P.mm(bk[6][:, 64 * hh:64 * (hh + 1)],
                 ctx["PTbf"].ap()[:, 128 * hh:128 * (hh + 1)],
                 ctx["Bt"].ap()[:, 64 * hh:64 * (hh + 1)],
                 start=True, stop=True)
        P.copy(ctx["Ubf"].ap(), bk[6][:, 0:512], engine="vector")
        for hh in range(8):
            b_oa = bk[hh % 2]
            b_ob = bk[2 + hh % 2]
            qcol = 128 * (hh // 2)
            P.mm(b_oa[0:64, qcol:qcol + 128],
                 ctx["Ubf"].ap()[:, 64 * hh:64 * (hh + 1)],
                 ctx["WTm"].ap()[:, 128 * hh:128 * (hh + 1)],
                 start=True, stop=True)
            P.mm(b_ob[0:64, qcol:qcol + 128],
                 ctx["Sbf"].ap()[:, 64 * hh:64 * (hh + 1)],
                 qk("q", hh), start=True, stop=True)
            eb = ctx["ebc8"].ap()[0:64, 128 * hh:128 * (hh + 1)]
            P.tt(ctx["tmA"].ap()[0:64, 0:128], b_oa[0:64, qcol:qcol + 128],
                 eb, ALU.mult)
            P.tt(ctx["tmB"].ap()[0:64, 0:128], b_ob[0:64, qcol:qcol + 128],
                 eb, ALU.mult)
            P.tt(ctx[f"och{par}"].ap()[:, 128 * hh:128 * (hh + 1)],
                 ctx["tmA"].ap()[0:64, 0:128], ctx["tmB"].ap()[0:64, 0:128],
                 ALU.add, engine="gpsimd")
        if ctx.get("dbg_chunk") == c:
            for nm in ("Bt", "Ubf", "KIT", "Sbf"):
                P.dma("gpsimd", ctx["dbg_" + nm].ap(), ctx[nm].ap())
        P.dma("gpsimd", ctx["o_dram"].ap()[64 * c:64 * (c + 1), :],
              ctx[f"och{par}"].ap())
        for hh in range(8):
            P.mm(bk[7][0:64, 64 * hh:64 * (hh + 1)],
                 ctx["KIT"].ap()[:, 64 * hh:64 * (hh + 1)],
                 ctx["Ubf"].ap()[:, 64 * hh:64 * (hh + 1)],
                 start=True, stop=True)
            P.tt(ctx["tmB"].ap()[0:64, 0:64],
                 ctx["Sst"].ap()[0:64, 64 * hh:64 * (hh + 1)],
                 bk[7][0:64, 64 * hh:64 * (hh + 1)], ALU.add)
            P.ts(ctx["Sst"].ap()[0:64, 64 * hh:64 * (hh + 1)],
                 ctx["tmB"].ap()[0:64, 0:64],
                 ctx["ebc8"].ap()[0:64, 128 * hh + 127:128 * hh + 128],
                 op0=ALU.mult)
            P.copy(ctx["Sbf"].ap()[0:64, 64 * hh:64 * (hh + 1)],
                   ctx["Sst"].ap()[0:64, 64 * hh:64 * (hh + 1)],
                   engine="scalar")


def build_launch2(dbg_chunk=None, dbg_early=False):
    nc, ctx, P = open_launch()
    stack = ExitStack()
    ctx["stack"] = stack
    with stack:
        ins = {}
        for nm, shape, dt in [("q_sc", [32 * 64, 1024], BF16),
                              ("k_sc", [32 * 64, 1024], BF16),
                              ("v_sc", [4096, 512], BF16),
                              ("beta_fl", [8, 4096], F32),
                              ("lna_fl", [8, 4096], F32)]:
            ins[nm] = nc.dram_tensor("i_" + nm, shape, dt, kind="ExternalInput")
            ctx["in_" + nm] = ins[nm]
        o_dram = nc.dram_tensor("o_scan", [32 * 64, 1024], BF16, kind="ExternalOutput")
        ctx["o_dram"] = o_dram
        ctx["dbg_chunk"] = dbg_chunk
        ctx["dbg_early"] = dbg_early
        if dbg_chunk is not None:
            for nm, shape, dt in [("Xa", [128, 1024], F32), ("XaT", [128, 1024], F32),
                                  ("PT", [128, 1024], F32), ("WTm", [128, 1024], BF16),
                                  ("en8", [128, 1024], BF16), ("ebc8", [128, 1024], F32),
                                  ("Bt", [128, 512], BF16), ("Ubf", [128, 512], BF16),
                                  ("KIT", [128, 512], BF16), ("Sbf", [64, 512], BF16)]:
                ctx["dbg_" + nm] = nc.dram_tensor("dbg_" + nm, shape, dt,
                                                  kind="ExternalOutput")
        alloc_banks(ctx)
        declare_consts(P, ctx, ["ones_row", "ident_bf", "ident_f32"])
        for nm, shape, dt in [("ones11_f32", [1, 1], F32),
                              ("cummask_f32", [128, 128], F32),
                              ("maskneg_rep_f", [128, 1024], F32),
                              ("maskli_rep", [128, 1024], BF16),
                              ("ident_rep_f", [128, 1024], F32)]:
            dr = nc.dram_tensor("c_" + nm, shape, dt, kind="ExternalInput")
            sb(ctx, nm, shape, dt)
            P.dma("sync", ctx[nm].ap(), dr.ap())
        for par in range(2):
            sb(ctx, f"qch{par}", [64, 1024], BF16)
            sb(ctx, f"kch{par}", [64, 1024], BF16)
            sb(ctx, f"och{par}", [64, 1024], BF16)
        sb(ctx, "v_sb", [128, 32 * 512], BF16)
        for c in range(32):
            P.dma("sync", ctx["v_sb"].ap()[:, 512 * c:512 * (c + 1)],
                  ins["v_sc"].ap()[128 * c:128 * (c + 1), :])
        emit_scan(P, ctx)
        finish_launch(nc, ctx, P)
    return nc


def scan_consts(core_id):
    t = np.arange(128)[:, None]
    s = np.arange(128)[None, :]
    c = host_consts(core_id)
    out = {"c_ones11_f32": np.ones((1, 1), np.float32),
           "c_cummask_f32": (t <= s).astype(np.float32),
           "c_maskneg_rep_f": np.tile(-((s < t).astype(np.float32)), (1, 8)),
           "c_maskli_rep": np.tile((t <= s).astype(BF), (1, 8)),
           "c_ident_rep_f": np.tile(np.eye(128, dtype=np.float32), (1, 8)),
           "c_ones_row": c["ones_row"], "c_ident_bf": c["ident_bf"],
           "c_ident_f32": c["ident_f32"]}
    return out


def assemble_scan_inputs(l1_results, core_id):
    b, hg = divmod(core_id, 2)
    r0, r1 = l1_results[2 * b], l1_results[2 * b + 1]
    q0, q1 = r0["o_q_dram"], r1["o_q_dram"]
    k0, k1 = r0["o_k_dram"], r1["o_k_dram"]
    v0, v1 = r0["o_v_dram"], r1["o_v_dram"]
    q_sc = np.empty((32 * 64, 1024), BF)
    k_sc = np.empty((32 * 64, 1024), BF)
    v_sc = np.empty((4096, 512), BF)
    beta = np.empty((8, 4096), np.float32)
    lna = np.empty((8, 4096), np.float32)
    for g in range(32):
        src = (q0, k0, v0, r0) if g % 2 == 0 else (q1, k1, v1, r1)
        j = g // 2
        cslice = slice(128 * j, 128 * (j + 1))
        gslice = slice(128 * g, 128 * (g + 1))
        for hh in range(8):
            hr = slice(512 * hg + 64 * hh, 512 * hg + 64 * (hh + 1))
            q_sc[64 * g:64 * (g + 1), 128 * hh:128 * (hh + 1)] = src[0][hr, cslice]
            k_sc[64 * g:64 * (g + 1), 128 * hh:128 * (hh + 1)] = src[1][hr, cslice]
        v_sc[gslice, :] = src[2][cslice, 512 * hg:512 * (hg + 1)]
        beta[:, gslice] = src[3]["o_beta_dram"][8 * hg:8 * hg + 8, cslice]
        lna[:, gslice] = src[3]["o_lna_dram"][8 * hg:8 * hg + 8, cslice]
    m = {"i_q_sc": q_sc, "i_k_sc": k_sc, "i_v_sc": v_sc,
         "i_beta_fl": beta, "i_lna_fl": lna}
    m.update(scan_consts(core_id))
    return m



def emit_rms_cols(P, ctx, src_tiles, dst_tiles, wcol_ap, eps, width=TLOC,
                  extra_sc=None):
    b_sq, b_s = ctx["bank0"].ap(), ctx["bank2"].ap()
    ones_col = ctx["ones_col"].ap()
    ones_row = ctx["ones_row"].ap()
    onesf = _ones_f(P, ctx)
    for cc in range(width // 512):
        w = 512
        c0 = cc * 512
        for fc in range(8):
            P.act(ctx["xsqf"].ap()[:, 0:w], src_tiles[fc][:, c0:c0 + w],
                  AF.Square)
            P.mm(b_sq[0:1, 0:w], onesf, ctx["xsqf"].ap()[:, 0:w],
                 start=(fc == 0), stop=(fc == 7))
        r_a = ctx["r_a"].ap()[0:1, 0:w]
        r_b = ctx["r_b"].ap()[0:1, 0:w]
        P.ts(r_a, b_sq[0:1, 0:w], 1.0 / D, s2=eps, op0=ALU.mult, op1=ALU.add)
        P.recip(r_b, r_a)
        P.act(r_b, r_b, AF.Sqrt)
        if extra_sc is not None:
            P.ts(r_b, r_b, extra_sc, op0=ALU.mult)
        P.copy(ctx["r_sbf"].ap()[0:1, 0:w], r_b, engine="vector")
        P.mm(b_s[:, 0:w], ones_row, ctx["r_sbf"].ap()[0:1, 0:w],
             start=True, stop=True)
        for fc in range(8):
            P.tt(ctx["tmpf"].ap()[:, 0:w], src_tiles[fc][:, c0:c0 + w],
                 b_s[:, 0:w], ALU.mult)
            P.ts(dst_tiles[fc][:, c0:c0 + w], ctx["tmpf"].ap()[:, 0:w],
                 wcol_ap[:, 0:1].tensor.ap()[:, 0:0 + 1] if False else wcol_ap[:, fc:fc + 1],
                 op0=ALU.mult)


def build_launch3():
    nc, ctx, P = open_launch()
    stack = ExitStack()
    ctx["stack"] = stack
    with stack:
        ins = {}
        for nm, shape, dt in [("xT", [D, TLOC], F32),
                              ("O", [D, TLOC], BF16),
                              ("wo", [D, D], F32),
                              ("n2w_col", [128, 8], F32),
                              ("wg", [22, 128, 1024], F32),
                              ("wu", [22, 128, 1024], F32),
                              ("wd", [L0_IP, D], F32)]:
            ins[nm] = nc.dram_tensor("i_" + nm, shape, dt, kind="ExternalInput")
            ctx["in_" + nm] = ins[nm]
        r2 = nc.dram_tensor("o_R2", [D, TLOC], F32, kind="ExternalOutput")
        alloc_banks(ctx)
        declare_consts(P, ctx, ["ones_col", "ones_row"])
        sb(ctx, "n2w_col", [128, 8], F32)
        P.dma("sync", ctx["n2w_col"].ap(), ins["n2w_col"].ap())
        for fc in range(8):
            sb(ctx, f"R{fc}", [128, TLOC], F32)
            P.dma("sync", ctx[f"R{fc}"].ap(),
                  ins["xT"].ap()[fc * 128:(fc + 1) * 128, :])
        sb(ctx, "tmpf", [128, 512], F32)
        hstack = ExitStack()
        hs_ctx = hstack
        for fc in range(8):
            ctx[f"hblk{fc}"] = hstack.enter_context(
                ctx["nc"].sbuf_tensor(f"hblk{fc}", [128, TLOC], F32, side="right"))
        with ExitStack() as tmpstack:
            ctx["tmpstack"] = tmpstack
            sb_tmp(ctx, "wob", [128, 8 * 1024], BF16)
            load_weight_bf16(P, ctx, ins["wo"], "wob", 8, 1024)
            for nm, shp, dt_ in (("xbf", [128, 512], BF16), ("xsq", [128, 512], BF16),
                                 ("r_a", [1, 512], F32), ("r_b", [1, 512], F32),
                                 ("r_v", [1, 512], F32), ("r_sbf", [1, 512], BF16),
                                 ("r_nbf", [1, 512], BF16)):
                ctx[nm] = ctx["tmpstack"].enter_context(
                    ctx["nc"].sbuf_tensor("mx_" + nm, shp, dt_, side="right"))
            emit_ln_plain(P, ctx, [ctx[f"R{fc}"].ap() for fc in range(8)],
                          [ctx[f"hblk{fc}"].ap() for fc in range(8)])
            for fc in range(8):
                sb_tmp(ctx, f"O{fc}", [128, TLOC], BF16)
                P.dma("sync", ctx[f"O{fc}"].ap(),
                      ins["O"].ap()[fc * 128:(fc + 1) * 128, :])
            for cc in range(4):
                for fc in range(8):
                    bkx = ctx[f"bank{fc % 2}"].ap()
                    for kc in range(8):
                        P.mm(bkx[:, 0:512],
                             ctx["wob"].ap()[:, kc * 1024 + fc * 128:kc * 1024 + (fc + 1) * 128],
                             ctx[f"O{kc}"].ap()[:, cc * 512:(cc + 1) * 512],
                             start=(kc == 0), stop=(kc == 7))
                    P.tt(ctx[f"hblk{fc}"].ap()[:, cc * 512:(cc + 1) * 512],
                         ctx[f"hblk{fc}"].ap()[:, cc * 512:(cc + 1) * 512],
                         bkx[:, 0:512], ALU.add)
                    P.tt(ctx[f"R{fc}"].ap()[:, cc * 512:(cc + 1) * 512],
                         ctx[f"R{fc}"].ap()[:, cc * 512:(cc + 1) * 512],
                         ctx[f"hblk{fc}"].ap()[:, cc * 512:(cc + 1) * 512],
                         ALU.add)
        P.fence()
        sb(ctx, "xbf", [128, 512], BF16)
        sb(ctx, "xsq", [128, 512], BF16)
        sb(ctx, "r_a", [1, 512], F32)
        sb(ctx, "r_b", [1, 512], F32)
        sb(ctx, "r_v", [1, 512], F32)
        sb(ctx, "r_sbf", [1, 512], BF16)
        for fc in range(8):
            sb(ctx, f"hn{fc}", [128, TLOC], BF16)
        emit_rms_cols(P, ctx, [ctx[f"hblk{fc}"].ap() for fc in range(8)],
                      [ctx[f"hn{fc}"].ap() for fc in range(8)],
                      ctx["n2w_col"].ap(), 1e-6)
        hstack.close()
        P.fence()
        sb(ctx, "wdb", [128, 22 * 1024], BF16)
        load_weight_bf16(P, ctx, ins["wd"], "wdb", 22, 1024)
        for par in range(2):
            sb(ctx, f"wgt{par}", [128, 1024], BF16)
            sb(ctx, f"wut{par}", [128, 1024], BF16)
        sb(ctx, "wldg", [128, 1024], F32)
        sb(ctx, "z_cc", [128, 22 * 512], BF16)
        sb(ctx, "tz1", [128, 512], BF16)
        for cc in range(4):
            for icc in range(22):
                par = icc % 2
                for wname, wt in (("wg", f"wgt{par}"), ("wu", f"wut{par}")):
                    P.dma("sync", ctx["wldg"].ap(), ins[wname].ap()[icc])
                    P.copy(ctx[wt].ap(), ctx["wldg"].ap(), engine="vector")
                bA, bB = ctx[f"bank{par * 2}"].ap(), ctx[f"bank{par * 2 + 1}"].ap()
                for kc in range(8):
                    P.mm(bA[:, 0:512],
                         ctx[f"wgt{par}"].ap()[:, kc * 128:(kc + 1) * 128],
                         ctx[f"hn{kc}"].ap()[:, cc * 512:(cc + 1) * 512],
                         start=(kc == 0), stop=(kc == 7))
                    P.mm(bB[:, 0:512],
                         ctx[f"wut{par}"].ap()[:, kc * 128:(kc + 1) * 128],
                         ctx[f"hn{kc}"].ap()[:, cc * 512:(cc + 1) * 512],
                         start=(kc == 0), stop=(kc == 7))
                P.act(ctx["tz1"].ap(), bA[:, 0:512], AF.Silu)
                P.tt(ctx["z_cc"].ap()[:, icc * 512:(icc + 1) * 512],
                     ctx["tz1"].ap(), bB[:, 0:512], ALU.mult)
            for fc in range(8):
                bD = ctx[f"bank{4 + fc % 2}"].ap()
                for icc in range(22):
                    P.mm(bD[:, 0:512],
                         ctx["wdb"].ap()[:, icc * 1024 + fc * 128:icc * 1024 + (fc + 1) * 128],
                         ctx["z_cc"].ap()[:, icc * 512:(icc + 1) * 512],
                         start=(icc == 0), stop=(icc == 21))
                P.tt(ctx[f"R{fc}"].ap()[:, cc * 512:(cc + 1) * 512],
                     ctx[f"R{fc}"].ap()[:, cc * 512:(cc + 1) * 512],
                     bD[:, 0:512], ALU.add)
        for fc in range(8):
            P.dma("gpsimd", r2.ap()[fc * 128:(fc + 1) * 128, :],
                  ctx[f"R{fc}"].ap())
        finish_launch(nc, ctx, P)
    return nc


def launch3_inputs(inputs, l1_results, l2_results, core_id):
    b, p = divmod(core_id, 2)
    xt, _ = shard_x(np.asarray(inputs["L0_attn_x"]), core_id)
    O = np.empty((D, TLOC), BF)
    for hg in range(2):
        od = l2_results[2 * b + hg]["o_scan"]
        for hl in range(8):
            for j in range(NLOC):
                g = 2 * j + p
                O[512 * hg + 64 * hl:512 * hg + 64 * (hl + 1),
                  128 * j:128 * (j + 1)] = od[64 * g:64 * (g + 1),
                                              128 * hl:128 * (hl + 1)]
    n2w = np.asarray(inputs["L0_attn_norm2_w"], np.float32)
    n2c = np.zeros((128, 8), np.float32)
    for fc in range(8):
        n2c[:, fc] = n2w[fc * 128:(fc + 1) * 128]
    wg = np.zeros((D, L0_IP), np.float32)
    wg[:, 0:L0_I] = np.asarray(inputs["L0_attn_wg"], np.float32)
    wu = np.zeros((D, L0_IP), np.float32)
    wu[:, 0:L0_I] = np.asarray(inputs["L0_attn_wu"], np.float32)
    wd = np.zeros((L0_IP, D), np.float32)
    wd[0:L0_I, :] = np.asarray(inputs["L0_attn_wd"], np.float32)
    wg_r = np.ascontiguousarray(wg.reshape(8, 128, 22, 128).transpose(2, 1, 0, 3).reshape(22, 128, 1024))
    wu_r = np.ascontiguousarray(wu.reshape(8, 128, 22, 128).transpose(2, 1, 0, 3).reshape(22, 128, 1024))
    c = host_consts(core_id)
    return {"i_xT": xt, "i_O": O,
            "i_wo": np.ascontiguousarray(np.asarray(inputs["L0_attn_wo"], np.float32)),
            "i_n2w_col": n2c, "i_wg": wg_r, "i_wu": wu_r, "i_wd": wd,
            "c_ones_col": c["ones_col"], "c_ones_row": c["ones_row"]}



def emit_ln_plain(P, ctx, src_tiles, dst_tiles, width=TLOC):
    b_sum, b_sq, b_s, b_m = (ctx["bank0"].ap(), ctx["bank1"].ap(),
                             ctx["bank2"].ap(), ctx["bank3"].ap())
    ones_col = ctx["ones_col"].ap()
    ones_row = ctx["ones_row"].ap()
    onesf = _ones_f(P, ctx)
    for cc in range(width // 512):
        c0, w = cc * 512, 512
        for fc in range(8):
            P.act(ctx["xsqf"].ap()[:, 0:w], src_tiles[fc][:, c0:c0 + w],
                  AF.Square)
            P.mm(b_sum[0:1, 0:w], onesf, src_tiles[fc][:, c0:c0 + w],
                 start=(fc == 0), stop=(fc == 7))
            P.mm(b_sq[0:1, 0:w], onesf, ctx["xsqf"].ap()[:, 0:w],
                 start=(fc == 0), stop=(fc == 7))
        r_a, r_b = ctx["r_a"].ap()[0:1, 0:w], ctx["r_b"].ap()[0:1, 0:w]
        r_v = ctx["r_v"].ap()[0:1, 0:w]
        P.ts(r_a, b_sum[0:1, 0:w], 1.0 / D, op0=ALU.mult)
        P.tt(r_b, r_a, r_a, ALU.mult)
        P.ts(r_v, b_sq[0:1, 0:w], 1.0 / D, op0=ALU.mult)
        P.tt(r_v, r_v, r_b, ALU.subtract)
        P.ts(r_v, r_v, 1e-5, op0=ALU.add)
        P.recip(r_b, r_v)
        P.act(r_b, r_b, AF.Sqrt)
        P.tt(r_a, r_a, r_b, ALU.mult)
        P.ts(r_a, r_a, -1.0, op0=ALU.mult)
        P.copy(ctx["r_sbf"].ap()[0:1, 0:w], r_b, engine="vector")
        P.copy(ctx["r_nbf"].ap()[0:1, 0:w], r_a, engine="vector")
        P.mm(b_s[:, 0:w], ones_row, ctx["r_sbf"].ap()[0:1, 0:w],
             start=True, stop=True)
        P.mm(b_m[:, 0:w], ones_row, ctx["r_nbf"].ap()[0:1, 0:w],
             start=True, stop=True)
        for fc in range(8):
            P.tt(ctx["tmpf"].ap()[:, 0:w], src_tiles[fc][:, c0:c0 + w],
                 b_s[:, 0:w], ALU.mult)
            P.tt(dst_tiles[fc][:, c0:c0 + w], ctx["tmpf"].ap()[:, 0:w],
                 b_m[:, 0:w], ALU.add)


def build_launch4():
    nc, ctx, P = open_launch()
    stack = ExitStack()
    ctx["stack"] = stack
    with stack:
        ins = {}
        for nm, shape, dt in [("R2", [D, TLOC], F32),
                              ("gw", [128, 64], F32), ("gb", [8, 1], F32),
                              ("w1", [8, 16, 128, 1024], F32),
                              ("w2", [8, 16, 128, 1024], F32),
                              ("wgm", [8, 16, 128, 2048], F32),
                              ("w3m", [8, 8, 128, 2048], F32),
                              ("b1c", [128, 8 * 16], F32),
                              ("b2c", [128, 8 * 16], F32),
                              ("bgc", [128, 8 * 16], F32),
                              ("b3c", [128, 8 * 8], F32),
                              ("sel", [8, 1024], BF16)]:
            ins[nm] = nc.dram_tensor("i_" + nm, shape, dt, kind="ExternalInput")
            ctx["in_" + nm] = ins[nm]
        r3 = nc.dram_tensor("o_R3", [D, TLOC], F32, kind="ExternalOutput")
        alloc_banks(ctx)
        declare_consts(P, ctx, ["ones_col", "ones_row", "ident_f32"])
        for nm in ("gw", "b1c", "b2c", "bgc", "b3c", "sel", "gb"):
            shp = {"gw": [128, 64], "b1c": [128, 8 * 16], "b2c": [128, 8 * 16],
                   "bgc": [128, 8 * 16], "b3c": [128, 8 * 8],
                   "sel": [8, 1024], "gb": [8, 1]}[nm]
            sb(ctx, nm, shp, F32 if nm != "sel" else BF16)
            P.dma("sync", ctx[nm].ap(), ins[nm].ap())
        sb(ctx, "gwb", [128, 64], BF16)
        P.copy(ctx["gwb"].ap(), ctx["gw"].ap(), engine="vector")
        for fc in range(8):
            sb(ctx, f"R{fc}", [128, TLOC], F32)
            P.dma("sync", ctx[f"R{fc}"].ap(),
                  ins["R2"].ap()[fc * 128:(fc + 1) * 128, :])
        sb(ctx, "tmpf", [128, 512], F32)
        for fc in range(8):
            sb(ctx, f"xm{fc}", [128, TLOC], BF16)
        gatestack = ExitStack()
        ctx["tmpstack"] = gatestack
        sb_tmp(ctx, "xbf", [128, 512], BF16)
        sb_tmp(ctx, "xsq", [128, 512], BF16)
        for nm in ("r_a", "r_b", "r_v"):
            sb_tmp(ctx, nm, [1, 512], F32)
        sb_tmp(ctx, "r_sbf", [1, 512], BF16)
        sb_tmp(ctx, "r_nbf", [1, 512], BF16)
        emit_ln_plain(P, ctx, [ctx[f"R{fc}"].ap() for fc in range(8)],
                      [ctx[f"xm{fc}"].ap() for fc in range(8)])
        sb_tmp(ctx, "lgscr", [128, TLOC], F32)
        sb(ctx, "wescr", [128, 128], F32)
        sb(ctx, "lt", [128, 8], F32)
        sb(ctx, "ltm", [128, 8], F32)
        sb(ctx, "m1", [128, 1], F32)
        sb(ctx, "m2", [128, 1], F32)
        sb(ctx, "p1", [128, 1], F32)
        sb(ctx, "p2", [128, 1], F32)
        sb(ctx, "eq1", [128, 8], F32)
        sb(ctx, "eq2", [128, 8], F32)
        sb(ctx, "weT", [8, TLOC], BF16)
        P.memset(ctx["wescr"].ap(), 0.0, engine="vector")
        for cc in range(4):
            bkL = ctx["bank4"].ap()
            for kc in range(8):
                P.mm(bkL[0:8, 0:512], ctx["gwb"].ap()[:, kc * 8:(kc + 1) * 8],
                     ctx[f"xm{kc}"].ap()[:, cc * 512:(cc + 1) * 512],
                     start=(kc == 0), stop=(kc == 7))
            P.ts(ctx["lgscr"].ap()[0:8, cc * 512:(cc + 1) * 512],
                 bkL[0:8, 0:512], ctx["gb"].ap(), op0=ALU.add)
        for tc in range(NLOC):
            bkT = ctx["bank5"].ap()
            P.transpose(bkT[:, 0:128], ctx["lgscr"].ap()[:, tc * 128:(tc + 1) * 128],
                        ctx["ident_f32"].ap())
            P.copy(ctx["lt"].ap(), bkT[:, 0:8], engine="vector")
            P.reduce(ctx["m1"].ap(), ctx["lt"].ap(), ALU.max)
            P.ts(ctx["eq1"].ap(), ctx["lt"].ap(), ctx["m1"].ap(), op0=ALU.is_equal)
            P.stt(ctx["ltm"].ap(), ctx["eq1"].ap(), NEG, ctx["lt"].ap(),
                  ALU.mult, ALU.add)
            P.reduce(ctx["m2"].ap(), ctx["ltm"].ap(), ALU.max)
            P.ts(ctx["eq2"].ap(), ctx["ltm"].ap(), ctx["m2"].ap(), op0=ALU.is_equal)
            P.tt(ctx["p1"].ap(), ctx["m1"].ap(), ctx["m2"].ap(), ALU.subtract)
            P.act(ctx["p1"].ap(), ctx["p1"].ap(), AF.Sigmoid)
            P.ts(ctx["p2"].ap(), ctx["p1"].ap(), -1.0, 1.0, op0=ALU.mult,
                 op1=ALU.add)
            P.ts(ctx["eq1"].ap(), ctx["eq1"].ap(), ctx["p1"].ap(), op0=ALU.mult)
            P.stt(ctx["wescr"].ap()[:, 0:8], ctx["eq2"].ap(), ctx["p2"].ap(),
                  ctx["eq1"].ap(), ALU.mult, ALU.add)
            P.transpose(bkT[:, 128:256], ctx["wescr"].ap(), ctx["ident_f32"].ap())
            P.copy(ctx["weT"].ap()[0:8, tc * 128:(tc + 1) * 128],
                   bkT[0:8, 128:256], engine="vector")
        gatestack.close()
        P.fence()
        sb(ctx, "z", [128, 16 * 1024], BF16)
        sb(ctx, "s_t", [128, 16 * 1024], BF16)
        sb(ctx, "x1t", [128, 512], BF16)
        sb(ctx, "x2t", [128, 512], BF16)
        sb(ctx, "webc", [128, 1024], BF16)
        for par in range(2):
            sb(ctx, f"w1fc{par}", [128, 1024], BF16)
            sb(ctx, f"w2fc{par}", [128, 1024], BF16)
            sb(ctx, f"wgfc{par}", [128, 2048], BF16)
            sb(ctx, f"w3fc{par}", [128, 2048], BF16)
        sb(ctx, "wld", [128, 1024], F32)
        for half in range(2):
            hoff = half * 1024
            for e in range(8):
                for c2 in range(2):
                    bkW = ctx["bank6"].ap()
                    P.mm(bkW[:, 0:512], ctx["sel"].ap()[:, e * 128:(e + 1) * 128],
                         ctx["weT"].ap()[:, hoff + c2 * 512:hoff + (c2 + 1) * 512],
                         start=True, stop=True)
                    P.copy(ctx["webc"].ap()[:, c2 * 512:(c2 + 1) * 512],
                           bkW[:, 0:512], engine="vector")
                for fc in range(16):
                    par = fc % 2
                    P.dma("sync", ctx["wld"].ap(), ins["w1"].ap()[e, fc])
                    P.copy(ctx[f"w1fc{par}"].ap(), ctx["wld"].ap(),
                           engine=("vector", "gpsimd")[par])
                    P.dma("sync", ctx["wld"].ap(), ins["w2"].ap()[e, fc])
                    P.copy(ctx[f"w2fc{par}"].ap(), ctx["wld"].ap(),
                           engine="scalar")
                    for c2 in range(2):
                        bA = ctx[f"bank{par * 2}"].ap()
                        bB = ctx[f"bank{par * 2 + 1}"].ap()
                        ccol = hoff + c2 * 512
                        for kc in range(8):
                            P.mm(bA[:, 0:512],
                                 ctx[f"w1fc{par}"].ap()[:, kc * 128:(kc + 1) * 128],
                                 ctx[f"xm{kc}"].ap()[:, ccol:ccol + 512],
                                 start=(kc == 0), stop=(kc == 7))
                            P.mm(bB[:, 0:512],
                                 ctx[f"w2fc{par}"].ap()[:, kc * 128:(kc + 1) * 128],
                                 ctx[f"xm{kc}"].ap()[:, ccol:ccol + 512],
                                 start=(kc == 0), stop=(kc == 7))
                        P.act(ctx["x1t"].ap(), bA[:, 0:512], AF.Identity,
                              bias=ctx["b1c"].ap()[:, e * 16 + fc:e * 16 + fc + 1])
                        P.act(ctx["x2t"].ap(), bB[:, 0:512], AF.Identity,
                              bias=ctx["b2c"].ap()[:, e * 16 + fc:e * 16 + fc + 1])
                        P.tt(ctx["z"].ap()[:, fc * 1024 + c2 * 512:fc * 1024 + (c2 + 1) * 512],
                             ctx["x1t"].ap(), ctx["x2t"].ap(), ALU.mult)
                for gc in range(16):
                    par = gc % 2
                    for hf in range(2):
                        P.dma("sync", ctx["wld"].ap(),
                              ins["wgm"].ap()[e, gc, :, hf * 1024:(hf + 1) * 1024])
                        P.copy(ctx[f"wgfc{par}"].ap()[:, hf * 1024:(hf + 1) * 1024],
                               ctx["wld"].ap(),
                               engine=("vector", "gpsimd", "scalar")[(2 * gc + hf) % 3])
                    for c2 in range(2):
                        bG = ctx[f"bank{4 + par}"].ap()
                        for zc in range(16):
                            P.mm(bG[:, 0:512],
                                 ctx[f"wgfc{par}"].ap()[:, zc * 128:(zc + 1) * 128],
                                 ctx["z"].ap()[:, zc * 1024 + c2 * 512:zc * 1024 + (c2 + 1) * 512],
                                 start=(zc == 0), stop=(zc == 15))
                        P.act(ctx["x1t"].ap(), bG[:, 0:512], AF.Silu,
                              bias=ctx["bgc"].ap()[:, e * 16 + gc:e * 16 + gc + 1])
                        P.tt(ctx["s_t"].ap()[:, gc * 1024 + c2 * 512:gc * 1024 + (c2 + 1) * 512],
                             ctx["x1t"].ap(),
                             ctx["webc"].ap()[:, c2 * 512:(c2 + 1) * 512],
                             ALU.mult)
                for fc in range(8):
                    par = fc % 2
                    for hf in range(2):
                        P.dma("sync", ctx["wld"].ap(),
                              ins["w3m"].ap()[e, fc, :, hf * 1024:(hf + 1) * 1024])
                        P.copy(ctx[f"w3fc{par}"].ap()[:, hf * 1024:(hf + 1) * 1024],
                               ctx["wld"].ap(),
                               engine=("vector", "gpsimd", "scalar")[(2 * fc + hf) % 3])
                    for c2 in range(2):
                        bD = ctx[f"bank{6 + par}"].ap()
                        ccol = hoff + c2 * 512
                        for gc in range(16):
                            P.mm(bD[:, 0:512],
                                 ctx[f"w3fc{par}"].ap()[:, gc * 128:(gc + 1) * 128],
                                 ctx["s_t"].ap()[:, gc * 1024 + c2 * 512:gc * 1024 + (c2 + 1) * 512],
                                 start=(gc == 0), stop=(gc == 15))
                        P.tt(ctx["tmpf"].ap(),
                             ctx[f"R{fc}"].ap()[:, ccol:ccol + 512],
                             bD[:, 0:512], ALU.add)
                        P.stt(ctx[f"R{fc}"].ap()[:, ccol:ccol + 512],
                              ctx["webc"].ap()[:, c2 * 512:(c2 + 1) * 512],
                              ctx["b3c"].ap()[:, e * 8 + fc:e * 8 + fc + 1],
                              ctx["tmpf"].ap(), ALU.mult, ALU.add)
        for fc in range(8):
            P.dma("gpsimd", r3.ap()[fc * 128:(fc + 1) * 128, :],
                  ctx[f"R{fc}"].ap())
        finish_launch(nc, ctx, P)
    return nc


def _colblk(w, nfc):
    E, K, F = w.shape
    kcs = K // 128
    r = w.reshape(E, kcs, 128, nfc, 128).transpose(0, 3, 2, 1, 4)
    return np.ascontiguousarray(r.reshape(E, nfc, 128, K))


def launch4_inputs(inputs, R2_by_core, core_id):
    gw = np.asarray(inputs["L0_ffn_gate_w"], np.float32)
    gw_r = np.ascontiguousarray(gw.reshape(8, 128, 8).transpose(1, 0, 2).reshape(128, 64))
    def colify(b, n):
        out = np.zeros((128, 8 * n), np.float32)
        for e in range(8):
            for fc in range(n):
                out[:, e * n + fc] = b[e, fc * 128:(fc + 1) * 128]
        return out
    sel = np.zeros((8, 1024), BF)
    for e in range(8):
        sel[e, e * 128:(e + 1) * 128] = 1
    c = host_consts(core_id)
    return {"i_R2": R2_by_core[core_id],
            "i_gw": gw_r,
            "i_gb": np.asarray(inputs["L0_ffn_gate_b"], np.float32).reshape(8, 1),
            "i_w1": _colblk(np.asarray(inputs["L0_ffn_w1"], np.float32), 16),
            "i_w2": _colblk(np.asarray(inputs["L0_ffn_w2"], np.float32), 16),
            "i_wgm": _colblk(np.asarray(inputs["L0_ffn_wg"], np.float32), 16),
            "i_w3m": _colblk(np.asarray(inputs["L0_ffn_w3"], np.float32), 8),
            "i_b1c": colify(np.asarray(inputs["L0_ffn_b1"], np.float32), 16),
            "i_b2c": colify(np.asarray(inputs["L0_ffn_b2"], np.float32), 16),
            "i_bgc": colify(np.asarray(inputs["L0_ffn_bg"], np.float32), 16),
            "i_b3c": colify(np.asarray(inputs["L0_ffn_b3"], np.float32), 8),
            "i_sel": sel,
            "c_ones_col": c["ones_col"], "c_ones_row": c["ones_row"],
            "c_ident_f32": c["ident_f32"]}



def build_launch5(phase=99, rmode=9):
    nc, ctx, P = open_launch()
    stack = ExitStack()
    ctx["stack"] = stack
    with stack:
        ins = {}
        for nm, shape, dt in [("R3", [D, TLOC], F32),
                              ("Wq", [D, D], F32), ("Wk", [D, 512], F32),
                              ("Wv", [D, 512], F32)]:
            ins[nm] = nc.dram_tensor("i_" + nm, shape, dt, kind="ExternalInput")
            ctx["in_" + nm] = ins[nm]
        outs = {}
        for nm, shape, dt in [("q_rot", [16 * 64, TLOC], BF16),
                              ("k_rot", [512, TLOC], BF16),
                              ("v_tm", [TLOC, 512], BF16),
                              ("qsq", [32, TLOC], F32)]:
            outs[nm] = nc.dram_tensor("o_" + nm, shape, dt, kind="ExternalOutput")
        alloc_banks(ctx)
        declare_consts(P, ctx, ["ones_col", "ones_row", "rope_perm",
                                "ropeCq", "ropeSq", "ropeCk", "ropeSk"] +
                       [f"pat32_{fc}" for fc in range(8)])
        for fc in range(8):
            sb(ctx, f"R{fc}", [128, TLOC], F32)
            P.dma("sync", ctx[f"R{fc}"].ap(),
                  ins["R3"].ap()[fc * 128:(fc + 1) * 128, :])
        sb(ctx, "tmpf", [128, 512], F32)
        gstack = ExitStack()
        ctx["tmpstack"] = gstack
        sb_tmp(ctx, "xbf", [128, 512], BF16)
        sb_tmp(ctx, "xsq", [128, 512], BF16)
        for nm in ("r_a", "r_b", "r_v"):
            sb_tmp(ctx, nm, [1, 512], F32)
        sb_tmp(ctx, "r_sbf", [1, 512], BF16)
        sb_tmp(ctx, "r_nbf", [1, 512], BF16)
        for fc in range(8):
            sb(ctx, f"xn{fc}", [128, TLOC], BF16)
        emit_ln_plain(P, ctx, [ctx[f"R{fc}"].ap() for fc in range(8)],
                      [ctx[f"xn{fc}"].ap() for fc in range(8)])
        gstack.close()
        P.fence()
        if phase < 2:
            finish_launch(nc, ctx, P)
            return nc
        sb(ctx, "wqb", [128, 8 * 1024], BF16)
        sb(ctx, "wkb", [128, 8 * 512], BF16)
        sb(ctx, "wvb", [128, 8 * 512], BF16)
        load_weight_bf16(P, ctx, ins["Wq"], "wqb", 8, 1024)
        load_weight_bf16(P, ctx, ins["Wk"], "wkb", 8, 512)
        load_weight_bf16(P, ctx, ins["Wv"], "wvb", 8, 512)
        if phase < 3:
            finish_launch(nc, ctx, P)
            return nc
        sb(ctx, "qf", [128, 512], F32)
        sb(ctx, "qb", [128, 512], BF16)
        sb(ctx, "t1", [128, 512], F32)
        sb(ctx, "t2", [128, 512], F32)
        sb(ctx, "rot", [128, 512], BF16)
        sb(ctx, "rsq", [128, 512], BF16)
        sb(ctx, "q32", [32, 512], F32)
        xn = [ctx[f"xn{fc}"] for fc in range(8)]
        for which, nfc, wname, outd, Ct, St in (
                ("q", 8, "wqb", outs["q_rot"], "ropeCq", "ropeSq"),
                ("k", 4, "wkb", outs["k_rot"], "ropeCk", "ropeSk")):
            w = ctx[wname].ap()
            wid = 1024 if which == "q" else 512
            for cc in range(4):
                bkS = ctx["bank4"].ap()
                for fc in range(nfc):
                    bkx = ctx[f"bank{fc % 2}"].ap()
                    bsw = ctx[f"bank{2 + fc % 2}"].ap()
                    for kc in range(8):
                        P.mm(bkx[:, 0:512],
                             w[:, kc * wid + fc * 128:kc * wid + (fc + 1) * 128],
                             xn[kc].ap()[:, cc * 512:(cc + 1) * 512],
                             start=(kc == 0), stop=(kc == 7))
                    P.copy(ctx["qf"].ap(), bkx[:, 0:512], engine="vector")
                    P.copy(ctx["qb"].ap(), bkx[:, 0:512], engine="scalar")
                    if rmode >= 1:
                        P.mm(bsw[:, 0:512], ctx["rope_perm"].ap(), ctx["qb"].ap(),
                             start=True, stop=True)
                    if rmode >= 2:
                        P.tt(ctx["t1"].ap(), ctx["qf"].ap(),
                             ctx[Ct].ap()[:, cc * 512:(cc + 1) * 512], ALU.mult)
                        P.tt(ctx["t2"].ap(), bsw[:, 0:512],
                             ctx[St].ap()[:, cc * 512:(cc + 1) * 512], ALU.mult)
                        P.tt(ctx["rot"].ap(), ctx["t1"].ap(), ctx["t2"].ap(),
                             ALU.add)
                    if rmode >= 3:
                        P.dma("gpsimd",
                              outd.ap()[fc * 128:(fc + 1) * 128,
                                        cc * 512:(cc + 1) * 512], ctx["rot"].ap())
                    if rmode >= 4 and which == "q":
                        P.tt(ctx["rsq"].ap(), ctx["rot"].ap(), ctx["rot"].ap(),
                             ALU.mult, engine="gpsimd")
                        P.mm(bkS[0:32, 0:512], ctx[f"pat32_{fc}"].ap(),
                             ctx["rsq"].ap(), start=(fc == 0), stop=(fc == nfc - 1))
                if rmode >= 4 and which == "q":
                    P.copy(ctx["q32"].ap(), bkS[0:32, 0:512], engine="vector")
                    P.dma("gpsimd",
                          outs["qsq"].ap()[:, cc * 512:(cc + 1) * 512],
                          ctx["q32"].ap())
        if phase < 4:
            finish_launch(nc, ctx, P)
            return nc
        for tc in range(NLOC):
            bkv = ctx[f"bank{tc % 2}"].ap()
            for kc in range(8):
                P.mm(bkv[:, 0:512], xn[kc].ap()[:, tc * 128:(tc + 1) * 128],
                     ctx["wvb"].ap()[:, kc * 512:(kc + 1) * 512],
                     start=(kc == 0), stop=(kc == 7))
            P.copy(ctx["rot"].ap(), bkv[:, 0:512], engine="vector")
            P.dma("gpsimd", outs["v_tm"].ap()[tc * 128:(tc + 1) * 128, :],
                  ctx["rot"].ap())
        finish_launch(nc, ctx, P)
    return nc


def launch5_inputs(inputs, R3_by_core, core_id):
    c = host_consts(core_id)
    m = {"i_R3": R3_by_core[core_id],
         "i_Wq": np.ascontiguousarray(np.asarray(inputs["L1_attn_Wq"], np.float32)),
         "i_Wk": np.ascontiguousarray(np.asarray(inputs["L1_attn_Wk"], np.float32)),
         "i_Wv": np.ascontiguousarray(np.asarray(inputs["L1_attn_Wv"], np.float32))}
    for nm in ("ones_col", "ones_row", "rope_perm", "ropeCq", "ropeSq",
               "ropeCk", "ropeSk"):
        m["c_" + nm] = c[nm]
    for fc in range(8):
        m[f"c_pat32_{fc}"] = c[f"pat32_{fc}"]
    return m



def build_launch6(parity, dbg=False):
    nc, ctx, P = open_launch()
    stack = ExitStack()
    ctx["stack"] = stack
    with stack:
        ins = {}
        for nm, shape, dt in [("R3", [D, TLOC], F32),
                              ("q_rot", [16, 64, TLOC], BF16),
                              ("qsq", [32, TLOC], F32),
                              ("k_all", [512, T], BF16),
                              ("v_aug", [T, 520], BF16),
                              ("woh", [16 * 64, D], F32),
                              ("lq1", [1, 32], F32), ("lk1", [1, 32], F32),
                              ("lq2", [1, 32], F32), ("lk2", [1, 32], F32),
                              ("rw64", [64, 1], F32),
                              ("kmap", [16, 32], F32)]:
            ins[nm] = nc.dram_tensor("i_" + nm, shape, dt, kind="ExternalInput")
            ctx["in_" + nm] = ins[nm]
        r4 = nc.dram_tensor("o_R4", [D, TLOC], F32, kind="ExternalOutput")
        negm_d = nc.dram_tensor("negm_d", [32, TLOC], BF16)
        alloc_banks(ctx)
        declare_consts(P, ctx, ["ones_col", "ones_row", "triu01_kq"])
        for fc in range(4):
            pat = nc.dram_tensor(f"c_patk_{fc}", [128, 16], BF16,
                                 kind="ExternalInput")
            sb(ctx, f"patk_{fc}", [128, 16], BF16)
            P.dma("sync", ctx[f"patk_{fc}"].ap(), pat.ap())
        sb(ctx, "onesrf", [1, 128], F32)
        P.memset(ctx["onesrf"].ap(), 1.0, engine="vector")
        astack = ExitStack()
        ctx["tmpstack"] = astack
        for i in range(4):
            sb_tmp(ctx, f"ka{i}", [128, T], BF16)
            P.dma("sync", ctx[f"ka{i}"].ap(),
                  ins["k_all"].ap()[128 * i:128 * (i + 1), :])
        sb_tmp(ctx, "v_aug", [128, 32 * 520], BF16)
        for kc in range(32):
            P.dma("sync", ctx["v_aug"].ap()[:, kc * 520:(kc + 1) * 520],
                  ins["v_aug"].ap()[kc * 128:(kc + 1) * 128, :])
        sb_tmp(ctx, "lrow", [1, 32], F32)
        sb_tmp(ctx, "lrow2", [1, 32], F32)
        sb_tmp(ctx, "lam", [1, 2], F32)
        sb(ctx, "neglam", [128, 1], F32)
        for i, (qn, kn) in enumerate((("lq1", "lk1"), ("lq2", "lk2"))):
            sb_tmp(ctx, f"lq{i}", [1, 32], F32)
            sb_tmp(ctx, f"lk{i}", [1, 32], F32)
            P.dma("sync", ctx[f"lq{i}"].ap(), ins[qn].ap())
            P.dma("sync", ctx[f"lk{i}"].ap(), ins[kn].ap())
            P.tt(ctx["lrow"].ap(), ctx[f"lq{i}"].ap(), ctx[f"lk{i}"].ap(),
                 ALU.mult)
            P.reduce(ctx["lam"].ap()[0:1, i:i + 1], ctx["lrow"].ap(), ALU.add)
            P.act(ctx["lam"].ap()[0:1, i:i + 1], ctx["lam"].ap()[0:1, i:i + 1],
                  AF.Exp)
        P.tt(ctx["lrow2"].ap()[0:1, 0:1], ctx["lam"].ap()[0:1, 0:1],
             ctx["lam"].ap()[0:1, 1:2], ALU.subtract)
        P.ts(ctx["lrow2"].ap()[0:1, 0:1], ctx["lrow2"].ap()[0:1, 0:1],
             -1.0, LAMBDA_INIT, op0=ALU.mult, op1=ALU.subtract)
        P.mm(ctx["bank7"].ap()[:, 0:1], ctx["onesrf"].ap(),
             ctx["lrow2"].ap()[0:1, 0:1], start=True, stop=True)
        P.copy(ctx["neglam"].ap(), ctx["bank7"].ap()[:, 0:1], engine="vector")
        sb(ctx, "rwcol", [64, 1], F32)
        P.dma("sync", ctx["rwcol"].ap(), ins["rw64"].ap())
        P.ts(ctx["rwcol"].ap(), ctx["rwcol"].ap(), 1.0 - LAMBDA_INIT,
             op0=ALU.mult)
        sb_tmp(ctx, "ksq", [128, 512], BF16)
        sb_tmp(ctx, "k16", [16, 512], F32)
        sb_tmp(ctx, "kmax", [16, 1], F32)
        sb_tmp(ctx, "kmap", [16, 32], F32)
        P.dma("sync", ctx["kmap"].ap(), ins["kmap"].ap())
        for cc8 in range(8):
            bkK = ctx["bank5"].ap()
            for fc in range(4):
                P.tt(ctx["ksq"].ap(), ctx[f"ka{fc}"].ap()[:, cc8 * 512:(cc8 + 1) * 512],
                     ctx[f"ka{fc}"].ap()[:, cc8 * 512:(cc8 + 1) * 512],
                     ALU.mult, engine="gpsimd")
                P.mm(bkK[0:16, 0:512], ctx[f"patk_{fc}"].ap(), ctx["ksq"].ap(),
                     start=(fc == 0), stop=(fc == 3))
            P.copy(ctx["k16"].ap(), bkK[0:16, 0:512], engine="vector")
            if cc8 == 0:
                P.reduce(ctx["kmax"].ap(), ctx["k16"].ap(), ALU.max)
            else:
                P.reduce(ctx["k16"].ap()[0:16, 0:1], ctx["k16"].ap(), ALU.max)
                P.tt(ctx["kmax"].ap(), ctx["kmax"].ap(),
                     ctx["k16"].ap()[0:16, 0:1], ALU.max)
        P.act(ctx["kmax"].ap(), ctx["kmax"].ap(), AF.Sqrt)
        sb_tmp(ctx, "kmq", [32, 1], F32)
        P.mm(ctx["bank7"].ap()[0:32, 4:5], ctx["kmap"].ap(), ctx["kmax"].ap(),
             start=True, stop=True)
        P.ts(ctx["kmq"].ap(), ctx["bank7"].ap()[0:32, 4:5], -1.0, op0=ALU.mult)
        sb_tmp(ctx, "qsq32", [32, TLOC], F32)
        sb_tmp(ctx, "negm", [32, TLOC], BF16)
        P.dma("sync", ctx["qsq32"].ap(), ins["qsq"].ap())
        P.act(ctx["qsq32"].ap(), ctx["qsq32"].ap(), AF.Sqrt)
        P.ts(ctx["negm"].ap(), ctx["qsq32"].ap(), ctx["kmq"].ap(), op0=ALU.mult)
        P.dma("gpsimd", negm_d.ap(), ctx["negm"].ap())
        for h2 in range(8):
            sb(ctx, f"attn{h2 * 2}", [64, TLOC], BF16)
            sb(ctx, f"attn{h2 * 2 + 1}", [64, TLOC], BF16)
        for par in range(2):
            sb_tmp(ctx, f"qh{par}", [64, TLOC], BF16)
            sb_tmp(ctx, f"kaug{par}", [33, T], BF16)
            P.memset(ctx[f"kaug{par}"].ap()[32:33, :], 1.0, engine="vector")
            sb_tmp(ctx, f"qa{par}", [33, TLOC], BF16)
        sb_tmp(ctx, "ptile", [128, 512], BF16)
        sb_tmp(ctx, "oat", [64, 512], F32)
        sb_tmp(ctx, "t0", [64, 512], F32)
        sb_tmp(ctx, "t1a", [64, 512], F32)
        sb_tmp(ctx, "den", [1, 512], F32)
        sb_tmp(ctx, "denb", [1, 512], BF16)
        sb_tmp(ctx, "r_r", [1, 512], F32)
        sb_tmp(ctx, "r_rb", [1, 512], BF16)
        sb_tmp(ctx, "sqh", [64, 512], BF16)
        for h in range(16):
            kh = h // 2
            P.dma("sync", ctx[f"qh{h % 2}"].ap(),
                  ins["q_rot"].ap()[h])
            for a in range(2):
                rk = 2 * kh + a
                r = 2 * h + a
                P.dma("sync", ctx[f"kaug{a}"].ap()[0:32, :],
                      ins["k_all"].ap()[32 * rk:32 * (rk + 1), :])
                P.copy(ctx[f"qa{a}"].ap()[0:32, :],
                       ctx[f"qh{h % 2}"].ap()[32 * a:32 * (a + 1), :],
                       engine="vector")
                P.dma("sync", ctx[f"qa{a}"].ap()[32:33, :],
                      negm_d.ap()[r:r + 1, :])
            for cc in range(4):
                for a in range(2):
                    bO = ctx[f"bank{a}"].ap()
                    nkc = 8 * cc + 7 + parity
                    for kc in range(nkc):
                        bS = ctx[f"bank{4 + kc % 2}"].ap()
                        c0 = 128 * max(0, -(-(kc - 8 * cc - parity) // 2))
                        P.mm(bS[:, c0:512],
                             ctx[f"kaug{a}"].ap()[:, kc * 128:(kc + 1) * 128],
                             ctx[f"qa{a}"].ap()[:, cc * 512 + c0:(cc + 1) * 512],
                             start=True, stop=True)
                        P.act(ctx["ptile"].ap()[:, c0:512], bS[:, c0:512], AF.Exp)
                        jd = (kc - parity) - 8 * cc
                        if jd >= 0 and jd % 2 == 0 and jd // 2 < 4:
                            j = jd // 2
                            P.tt(ctx["ptile"].ap()[:, j * 128:(j + 1) * 128],
                                 ctx["ptile"].ap()[:, j * 128:(j + 1) * 128],
                                 ctx["triu01_kq"].ap(), ALU.mult,
                                 engine="gpsimd")
                        P.mm(bO[0:65, c0:512],
                             ctx["v_aug"].ap()[:, kc * 520 + 65 * kh:kc * 520 + 65 * kh + 65],
                             ctx["ptile"].ap()[:, c0:512],
                             start=(kc == 0), stop=(kc == nkc - 1))
                    P.copy(ctx["den"].ap(), bO[64:65, 0:512], engine="vector")
                    P.recip(ctx["den"].ap(), ctx["den"].ap())
                    P.copy(ctx["denb"].ap(), ctx["den"].ap(), engine="vector")
                    bR = ctx["bank6"].ap()
                    P.mm(bR[:, 0:512], ctx["ones_row"].ap(), ctx["denb"].ap(),
                         start=True, stop=True)
                    P.copy(ctx["oat"].ap(), bO[0:64, 0:512], engine="scalar")
                    P.tt(ctx["t1a" if a else "t0"].ap(), ctx["oat"].ap(),
                         bR[0:64, 0:512], ALU.mult)
                P.stt(ctx["t1a"].ap(), ctx["t1a"].ap(),
                      ctx["neglam"].ap()[0:64, :], ctx["t0"].ap(),
                      ALU.mult, ALU.add)
                P.copy(ctx["sqh"].ap(), ctx["t1a"].ap(), engine="vector")
                P.tt(ctx["sqh"].ap(), ctx["sqh"].ap(), ctx["sqh"].ap(),
                     ALU.mult, engine="gpsimd")
                bM = ctx["bank7"].ap()
                P.mm(bM[0:1, 0:512], ctx["ones_col"].ap()[0:64, :],
                     ctx["sqh"].ap(), start=True, stop=True)
                P.ts(ctx["r_r"].ap(), bM[0:1, 0:512], 1.0 / 64,
                     1e-6, op0=ALU.mult, op1=ALU.add)
                P.recip(ctx["r_r"].ap(), ctx["r_r"].ap())
                P.act(ctx["r_r"].ap(), ctx["r_r"].ap(), AF.Sqrt)
                P.copy(ctx["r_rb"].ap(), ctx["r_r"].ap(), engine="vector")
                bR = ctx["bank6"].ap()
                P.mm(bR[:, 0:512], ctx["ones_row"].ap(), ctx["r_rb"].ap(),
                     start=True, stop=True)
                P.tt(ctx["oat"].ap(), ctx["t1a"].ap(), bR[0:64, 0:512],
                     ALU.mult)
                P.ts(ctx[f"attn{h}"].ap()[:, cc * 512:(cc + 1) * 512],
                     ctx["oat"].ap(), ctx["rwcol"].ap(), op0=ALU.mult)
        if dbg:
            for h in range(16):
                dt_ = nc.dram_tensor(f"dbg_attn{h}", [64, TLOC], BF16,
                                     kind="ExternalOutput")
                P.dma("gpsimd", dt_.ap(), ctx[f"attn{h}"].ap())
            dn = nc.dram_tensor("dbg_negm", [32, TLOC], BF16, kind="ExternalOutput")
            P.dma("gpsimd", dn.ap(), ctx["negm"].ap())
        astack.close()
        P.fence()
        sb(ctx, "wohb", [64, 16 * 1024], BF16)
        sb(ctx, "wldw", [64, 1024], F32)
        for h in range(16):
            P.dma("sync", ctx["wldw"].ap(),
                  ins["woh"].ap()[64 * h:64 * (h + 1), :])
            P.copy(ctx["wohb"].ap()[:, h * 1024:(h + 1) * 1024],
                   ctx["wldw"].ap(), engine=("vector", "gpsimd", "scalar")[h % 3])
        sb(ctx, "Rt", [128, 512], F32)
        sb(ctx, "R4t", [128, 512], F32)
        for cc in range(4):
            for fc in range(8):
                bW = ctx[f"bank{fc % 2}"].ap()
                for h in range(16):
                    P.mm(bW[:, 0:512],
                         ctx["wohb"].ap()[:, h * 1024 + fc * 128:h * 1024 + (fc + 1) * 128],
                         ctx[f"attn{h}"].ap()[:, cc * 512:(cc + 1) * 512],
                         start=(h == 0), stop=(h == 15))
                P.dma("sync", ctx["Rt"].ap(),
                      ins["R3"].ap()[fc * 128:(fc + 1) * 128,
                                     cc * 512:(cc + 1) * 512])
                P.tt(ctx["R4t"].ap(), ctx["Rt"].ap(), bW[:, 0:512], ALU.add)
                P.dma("gpsimd",
                      r4.ap()[fc * 128:(fc + 1) * 128, cc * 512:(cc + 1) * 512],
                      ctx["R4t"].ap())
        finish_launch(nc, ctx, P)
    return nc


def launch6_inputs(inputs, R3_by_core, l5_results, core_id):
    b, p = divmod(core_id, 2)
    r0, r1 = l5_results[2 * b], l5_results[2 * b + 1]
    k_all = np.empty((512, T), BF)
    v_aug = np.ones((T, 520), BF)
    for g in range(32):
        src = r0 if g % 2 == 0 else r1
        j = g // 2
        cs = slice(128 * j, 128 * (j + 1))
        gs = slice(128 * g, 128 * (g + 1))
        k_all[:, gs] = src["o_k_rot"][:, cs]
        v = src["o_v_tm"][cs, :]
        for kh in range(8):
            v_aug[gs, 65 * kh:65 * kh + 64] = v[:, 64 * kh:64 * (kh + 1)]
    me = l5_results[core_id]
    q_rot = me["o_q_rot"].reshape(16, 64, TLOC)
    woh = np.ascontiguousarray(
        np.asarray(inputs["L1_attn_Wo"], np.float32))
    kmap = np.zeros((16, 32), np.float32)
    for r in range(32):
        h = r // 2
        a = r % 2
        kmap[2 * (h // 2) + a, r] = 1
    c = host_consts(core_id)
    m = {"i_R3": R3_by_core[core_id], "i_q_rot": q_rot,
         "i_qsq": me["o_qsq"], "i_k_all": k_all, "i_v_aug": v_aug,
         "i_woh": woh,
         "i_lq1": np.asarray(inputs["L1_attn_lambda_q1"], np.float32).reshape(1, 32),
         "i_lk1": np.asarray(inputs["L1_attn_lambda_k1"], np.float32).reshape(1, 32),
         "i_lq2": np.asarray(inputs["L1_attn_lambda_q2"], np.float32).reshape(1, 32),
         "i_lk2": np.asarray(inputs["L1_attn_lambda_k2"], np.float32).reshape(1, 32),
         "i_rw64": np.asarray(inputs["L1_attn_rms_w"], np.float32).reshape(64, 1),
         "i_kmap": kmap,
         "c_ones_col": c["ones_col"], "c_ones_row": c["ones_row"],
         "c_triu01_kq": c["triu01_kq"]}
    for fc in range(4):
        pat = np.zeros((128, 16), BF)
        for pp in range(128):
            pat[pp, (fc * 128 + pp) // 32] = 1
        m[f"c_patk_{fc}"] = pat
    return m



def build_launch7():
    nc, ctx, P = open_launch()
    stack = ExitStack()
    ctx["stack"] = stack
    SC = 1.0 / 127.0
    with stack:
        ins = {}
        for nm, shape, dt in [("R4", [D, TLOC], F32),
                              ("wgT", [24, 128, 1024], I32),
                              ("wuT", [24, 128, 1024], I32),
                              ("wdT", [L1_FF, D], I32)]:
            ins[nm] = nc.dram_tensor("i_" + nm, shape, dt, kind="ExternalInput")
            ctx["in_" + nm] = ins[nm]
        r5 = nc.dram_tensor("o_R5", [D, TLOC], F32, kind="ExternalOutput")
        alloc_banks(ctx)
        declare_consts(P, ctx, ["ones_col", "ones_row"])
        for fc in range(8):
            sb(ctx, f"R{fc}", [128, TLOC], F32)
            P.dma("sync", ctx[f"R{fc}"].ap(),
                  ins["R4"].ap()[fc * 128:(fc + 1) * 128, :])
        sb(ctx, "tmpf", [128, 512], F32)
        gstack = ExitStack()
        ctx["tmpstack"] = gstack
        sb_tmp(ctx, "xbf", [128, 512], BF16)
        sb_tmp(ctx, "xsq", [128, 512], BF16)
        for nm in ("r_a", "r_b", "r_v"):
            sb_tmp(ctx, nm, [1, 512], F32)
        sb_tmp(ctx, "r_sbf", [1, 512], BF16)
        sb_tmp(ctx, "r_nbf", [1, 512], BF16)
        for fc in range(8):
            sb(ctx, f"xn{fc}", [128, TLOC], BF16)
        emit_ln_plain(P, ctx, [ctx[f"R{fc}"].ap() for fc in range(8)],
                      [ctx[f"xn{fc}"].ap() for fc in range(8)])
        gstack.close()
        P.fence()
        sb(ctx, "wdb", [128, 24 * 1024], BF16)
        sb(ctx, "wldi", [128, 1024], I32)
        for hc in range(24):
            P.dma("sync", ctx["wldi"].ap(),
                  ins["wdT"].ap()[hc * 128:(hc + 1) * 128, :])
            P.copy(ctx["wdb"].ap()[:, hc * 1024:(hc + 1) * 1024],
                   ctx["wldi"].ap(), engine=("vector", "gpsimd", "scalar")[hc % 3])
        for par in range(2):
            sb(ctx, f"wgt{par}", [128, 1024], BF16)
            sb(ctx, f"wut{par}", [128, 1024], BF16)
        sb(ctx, "z_cc", [128, 24 * 512], BF16)
        sb(ctx, "tz1", [128, 512], BF16)
        xn = [ctx[f"xn{fc}"] for fc in range(8)]
        for cc in range(4):
            for hc in range(24):
                par = hc % 2
                P.dma("sync", ctx["wldi"].ap(), ins["wgT"].ap()[hc])
                P.copy(ctx[f"wgt{par}"].ap(), ctx["wldi"].ap(),
                       engine=("vector", "gpsimd")[hc % 2])
                P.dma("sync", ctx["wldi"].ap(), ins["wuT"].ap()[hc])
                P.copy(ctx[f"wut{par}"].ap(), ctx["wldi"].ap(), engine="scalar")
                bG = ctx[f"bank{par * 2}"].ap()
                bU = ctx[f"bank{par * 2 + 1}"].ap()
                for kc in range(8):
                    P.mm(bG[:, 0:512],
                         ctx[f"wgt{par}"].ap()[:, kc * 128:(kc + 1) * 128],
                         xn[kc].ap()[:, cc * 512:(cc + 1) * 512],
                         start=(kc == 0), stop=(kc == 7))
                    P.mm(bU[:, 0:512],
                         ctx[f"wut{par}"].ap()[:, kc * 128:(kc + 1) * 128],
                         xn[kc].ap()[:, cc * 512:(cc + 1) * 512],
                         start=(kc == 0), stop=(kc == 7))
                P.act(ctx["tz1"].ap(), bG[:, 0:512], AF.Silu, scale=SC)
                P.stt(ctx["z_cc"].ap()[:, hc * 512:(hc + 1) * 512],
                      bU[:, 0:512], SC, ctx["tz1"].ap(), ALU.mult, ALU.mult)
            for fc in range(8):
                bD = ctx[f"bank{4 + fc % 2}"].ap()
                for hc in range(24):
                    P.mm(bD[:, 0:512],
                         ctx["wdb"].ap()[:, hc * 1024 + fc * 128:hc * 1024 + (fc + 1) * 128],
                         ctx["z_cc"].ap()[:, hc * 512:(hc + 1) * 512],
                         start=(hc == 0), stop=(hc == 23))
                P.stt(ctx[f"R{fc}"].ap()[:, cc * 512:(cc + 1) * 512],
                      bD[:, 0:512], SC,
                      ctx[f"R{fc}"].ap()[:, cc * 512:(cc + 1) * 512],
                      ALU.mult, ALU.add)
        for fc in range(8):
            P.dma("gpsimd", r5.ap()[fc * 128:(fc + 1) * 128, :],
                  ctx[f"R{fc}"].ap())
        finish_launch(nc, ctx, P)
    return nc


def launch7_inputs(inputs, R4_by_core, core_id):
    wg = np.asarray(inputs["L1_ffn_w_gate"])
    wu = np.asarray(inputs["L1_ffn_w_up"])
    wd = np.asarray(inputs["L1_ffn_w_down"])
    def blk(w):
        r = w.reshape(24, 128, 8, 128).transpose(0, 3, 2, 1)
        r = w.reshape(24, 128, 8, 128)
        r = r.transpose(0, 3, 2, 1)
        return np.ascontiguousarray(r.reshape(24, 128, 1024))
    c = host_consts(core_id)
    return {"i_R4": R4_by_core[core_id],
            "i_wgT": blk(wg), "i_wuT": blk(wu),
            "i_wdT": np.ascontiguousarray(wd.T),
            "c_ones_col": c["ones_col"], "c_ones_row": c["ones_row"]}



def kernel(**inputs):
    inputs = {k: np.asarray(v) for k, v in inputs.items()}
    nc1 = build_launch1()
    res1 = run_bass_kernel_spmd(
        nc1, [launch1_inputs(inputs, c) for c in range(8)],
        core_ids=list(range(8))).results
    nc2 = build_launch2()
    res2 = run_bass_kernel_spmd(
        nc2, [assemble_scan_inputs(res1, c) for c in range(8)],
        core_ids=list(range(8))).results
    nc3 = build_launch3()
    res3 = run_bass_kernel_spmd(
        nc3, [launch3_inputs(inputs, res1, res2, c) for c in range(8)],
        core_ids=list(range(8))).results
    R2 = {c: res3[c]["o_R2"] for c in range(8)}
    nc4 = build_launch4()
    res4 = run_bass_kernel_spmd(
        nc4, [launch4_inputs(inputs, R2, c) for c in range(8)],
        core_ids=list(range(8))).results
    R3 = {c: res4[c]["o_R3"] for c in range(8)}
    nc5 = build_launch5()
    res5 = run_bass_kernel_spmd(
        nc5, [launch5_inputs(inputs, R3, c) for c in range(8)],
        core_ids=list(range(8))).results
    res6 = [None] * 8
    for par in range(2):
        nc6 = build_launch6(par)
        cids = [2 * b + par for b in range(4)]
        rr = run_bass_kernel_spmd(
            nc6, [launch6_inputs(inputs, R3, res5, c) for c in cids],
            core_ids=cids).results
        for i, c in enumerate(cids):
            res6[c] = rr[i]
    R4 = {c: res6[c]["o_R4"] for c in range(8)}
    nc7 = build_launch7()
    res7 = run_bass_kernel_spmd(
        nc7, [launch7_inputs(inputs, R4, c) for c in range(8)],
        core_ids=list(range(8))).results
    out = np.empty((B, T, D), np.float32)
    for cid in range(8):
        b, p = divmod(cid, 2)
        r5 = res7[cid]["o_R5"]
        out[b, tok_index(p), :] = r5.T
    return out
